# version 17
# baseline (speedup 1.0000x reference)
import os
from collections import defaultdict
from contextlib import ExitStack

import numpy as np

import concourse.bass as bass
import concourse.mybir as mybir
from concourse.bass_utils import run_bass_kernel_spmd

F32 = mybir.dt.float32
BF16 = mybir.dt.bfloat16
AF = mybir.ActivationFunctionType
ALU = mybir.AluOpType
AX = mybir.AxisListType

D = 1024
DC = 8
NCTX = 4096
NSLOT = 16
NSS = 4
HPN = 8
E = 12
HF = 4096
HCN = 32
EPS = 1e-5
NEG = -1e30

SLOTS_EVEN = [0, 3, 4, 7, 8, 11, 12, 15, 16, 19, 20, 23, 24, 27, 28, 31]
SLOTS_ODD = [1, 2, 5, 6, 9, 10, 13, 14, 17, 18, 21, 22, 25, 26, 29, 30]


def klen(s):
    return 256 * (s + 1)


def cdiv(a, b):
    return -(-a // b)


class Walk:

    def __init__(self, role, eng, cnt):
        self.role = role
        self.eng = eng
        self.cnt = cnt

    def do(self, role, fn, inc=None, n=1):
        if self.role == role:
            inst = fn(self.eng)
            if inc is not None and inst is not None:
                inst.then_inc(inc, n)
        if inc is not None:
            self.cnt[inc.name] += n
            return self.cnt[inc.name]
        return None

    def wait(self, role, sem, val=None):
        if val is None:
            val = self.cnt[sem.name]
        if val <= 0:
            return
        if self.role == role:
            self.eng.wait_ge(sem, val)

    def v(self, sem):
        return self.cnt[sem.name]


def build_nc():
    nc = bass.Bass("TRN2", target_bir_lowering=False,
                   detect_race_conditions=(os.environ.get("MK_RACE", "0") == "1"))

    x_own_t = nc.dram_tensor("x_own", [NSLOT, 128, D], F32, kind="ExternalInput")
    x_ctx_t = nc.dram_tensor("x_ctx", [NCTX, D], F32, kind="ExternalInput")
    rowoff_t = nc.dram_tensor("rowoff", [1, NSLOT], F32, kind="ExternalInput")
    rowoffn_t = nc.dram_tensor("rowoffn", [1, NSLOT], F32, kind="ExternalInput")
    Wq_t = nc.dram_tensor("Wq", [D, D], F32, kind="ExternalInput")
    Wk_t = nc.dram_tensor("Wk", [D, D], F32, kind="ExternalInput")
    Wv_t = nc.dram_tensor("Wv", [D, D], F32, kind="ExternalInput")
    Wo_t = nc.dram_tensor("Wo", [D, D], F32, kind="ExternalInput")
    bq_t = nc.dram_tensor("bq", [1, D], F32, kind="ExternalInput")
    bk_t = nc.dram_tensor("bk", [1, D], F32, kind="ExternalInput")
    bv_t = nc.dram_tensor("bv", [1, D], F32, kind="ExternalInput")
    bo_t = nc.dram_tensor("bo", [1, D], F32, kind="ExternalInput")
    Wg_t = nc.dram_tensor("Wg", [D, E], F32, kind="ExternalInput")
    Wup_t = nc.dram_tensor("Wup", [E, D, HF], F32, kind="ExternalInput")
    bup_t = nc.dram_tensor("bup", [E, HF], F32, kind="ExternalInput")
    Wdn_t = nc.dram_tensor("Wdn", [E, HF, D], F32, kind="ExternalInput")
    bdn_t = nc.dram_tensor("bdn", [E, D], F32, kind="ExternalInput")
    out_t = nc.dram_tensor("out", [NSLOT, 128, D], F32, kind="ExternalOutput")

    _scratch_kind = "ExternalOutput" if os.environ.get("MK_DUMP") else "Internal"
    Kscr = nc.dram_tensor("Kscr", [HPN, 128, NCTX], BF16, kind=_scratch_kind)
    Vscr = nc.dram_tensor("Vscr", [HPN, NCTX, 128], BF16, kind=_scratch_kind)
    Qscr = nc.dram_tensor("Qscr", [HPN, 128, NSLOT * 128], BF16, kind=_scratch_kind)
    XNscr = nc.dram_tensor("XNscr", [NSLOT, DC, 128, 128], F32, kind=_scratch_kind)
    XNBFscr = nc.dram_tensor("XNBFscr", [NSLOT, DC, 128, 128], BF16, kind=_scratch_kind)
    _dbg2 = os.environ.get("MK_DBG2")
    _dbg3 = os.environ.get("MK_DBG3")
    if _dbg3:
        ACCdbg = nc.dram_tensor("ACCdbg", [DC, 128, 1024], F32, kind="ExternalOutput")
        CWdbg = nc.dram_tensor("CWdbg", [16, 2048], BF16, kind="ExternalOutput")
        HTdbg = nc.dram_tensor("HTdbg", [4, 128, 1024], BF16, kind="ExternalOutput")
    if _dbg2:
        AOdbg = nc.dram_tensor("AOdbg", [DC, 128, 512], BF16, kind="ExternalOutput")
        POdbg = nc.dram_tensor("POdbg", [4, 128, D], F32, kind="ExternalOutput")
        REdbg = nc.dram_tensor("REdbg", [4, 128, D], F32, kind="ExternalOutput")
        Pdbg = nc.dram_tensor("Pdbg", [2, 128, NCTX], BF16, kind="ExternalOutput")
    WUPscr = nc.dram_tensor("WUPscr", [E, 8, 128, 4096], BF16, kind="Internal")
    WDNscr = nc.dram_tensor("WDNscr", [E, 8, 128, 4096], BF16, kind="Internal")

    x_own = x_own_t.ap()
    x_ctx = x_ctx_t.ap()
    out_ap = out_t.ap()

    ctx = ExitStack()

    def sb(name, shape, dt=F32):
        return ctx.enter_context(nc.sbuf_tensor(name, shape, dt))

    def sem(name):
        return ctx.enter_context(nc.semaphore(name))

    with ctx:
        s_pe = sem("s_pe")
        s_act = sem("s_act")
        s_dve = sem("s_dve")
        s_pool = sem("s_pool")
        s_pro = sem("s_pro")
        s_bias = sem("s_bias")
        s_wslab = sem("s_wslab")
        s_gx = [sem(f"s_gx{i}") for i in range(2)]
        s_kst = [sem(f"s_kst{i}") for i in range(2)]
        s_vst = [sem(f"s_vst{i}") for i in range(2)]
        s_kr = [sem(f"s_kr{i}") for i in range(2)]
        s_xo = sem("s_xo")
        s_bo = sem("s_bo")
        s_wo = sem("s_wo")
        s_xnf = sem("s_xnf")
        s_xnb = sem("s_xnb")
        s_xnt = sem("s_xnt")
        s_acc = sem("s_acc")
        s_ws = sem("s_ws")
        s_upg = [sem(f"s_upg{i}") for i in range(3)]
        s_upr = [sem(f"s_upr{i}") for i in range(3)]
        s_dng = [sem(f"s_dng{i}") for i in range(3)]
        s_dnr = [sem(f"s_dnr{i}") for i in range(3)]
        s_cw = sem("s_cw")
        s_out = sem("s_out")

        cnt0 = defaultdict(int)

        ident_bf = sb("ident_bf", [128, 128], BF16)
        identf = sb("identf", [128, 128], F32)
        ones_bf = sb("ones_bf", [1, 128], BF16)
        rowoff_bc = sb("rowoff_bc", [128, NSLOT], F32)
        rowoffn_bc = sb("rowoffn_bc", [128, NSLOT], F32)
        eps_sb = sb("eps_sb", [128, 1], F32)
        bup_c = sb("bup_c", [128, E * 32], F32)
        bdn_c = sb("bdn_c", [128, E * 8], F32)
        wg_sb = sb("wg_sb", [128, DC * E], F32)

        phase_state = [dict(cnt0)]

        def run_phase(phase_fn):
            results = {}
            start = dict(phase_state[0])
            with nc.Block() as blk:
                for role, decor in (("sync", blk.sync), ("tensor", blk.tensor),
                                    ("scalar", blk.scalar), ("vector", blk.vector),
                                    ("gpsimd", blk.gpsimd)):
                    def make(role=role):
                        def f(eng):
                            w = Walk(role, eng, defaultdict(int, dict(start)))
                            phase_fn(w)
                            results[role] = dict(w.cnt)
                        return f
                    decor(make())
            vals = list(results.values())
            for v in vals[1:]:
                assert v == vals[0], f"walk divergence: {v} vs {vals[0]}"
            phase_state[0] = vals[0]

        p1 = ExitStack()
        bv_bc_f = p1.enter_context(nc.sbuf_tensor("bv_bc_f", [128, D], F32))
        bv_bc = p1.enter_context(nc.sbuf_tensor("bv_bc", [128, D], BF16))
        biasr = p1.enter_context(nc.sbuf_tensor("biasr", [128, 128], F32))
        bq_c = p1.enter_context(nc.sbuf_tensor("bq_c", [128, 8], F32))
        bk_c = p1.enter_context(nc.sbuf_tensor("bk_c", [128, 8], F32))
        wq_sb = [p1.enter_context(nc.sbuf_tensor(f"wq{c}", [128, D], BF16)) for c in range(DC)]
        wk_sb = [p1.enter_context(nc.sbuf_tensor(f"wk{c}", [128, D], BF16)) for c in range(DC)]
        wv_sb = [p1.enter_context(nc.sbuf_tensor(f"wv{c}", [128, D], BF16)) for c in range(DC)]
        xt = [[p1.enter_context(nc.sbuf_tensor(f"xt{p}_{j}", [128, D], BF16))
               for j in range(4)] for p in range(2)]
        Xg = [[p1.enter_context(nc.sbuf_tensor(f"Xg{p}_{c}", [128, 512], BF16))
               for c in range(DC)] for p in range(2)]
        Ksta = [p1.enter_context(nc.sbuf_tensor(f"Ksta{p}", [128, 512], BF16)) for p in range(2)]
        Vsta = [p1.enter_context(nc.sbuf_tensor(f"Vsta{p}", [128, 512], BF16)) for p in range(2)]
        Vstb = [p1.enter_context(nc.sbuf_tensor(f"Vstb{p}", [128, 512], BF16)) for p in range(2)]
        ps_f = p1.enter_context(nc.psum_tensor("ps_f", [128, 6 * 512], F32))
        ps_t = [p1.enter_context(nc.psum_tensor(f"ps_t{i}", [128, 512], BF16)) for i in range(2)]

        def phase1(w):
            cap = {}
            w.do("sync", lambda e: e.dma_start(
                out=rowoff_bc[:], in_=bass.AP(rowoff_t, 0, [[0, 128], [1, NSLOT]])), s_pro, 16)
            w.do("sync", lambda e: e.dma_start(
                out=rowoffn_bc[:], in_=bass.AP(rowoffn_t, 0, [[0, 128], [1, NSLOT]])), s_pro, 16)
            w.do("sync", lambda e: e.dma_start(
                out=bv_bc_f[:], in_=bass.AP(bv_t, 0, [[0, 128], [1, D]])), s_pro, 16)
            cap["pro_sy"] = w.v(s_pro)

            w.do("gpsimd", lambda e: e.memset(ident_bf[:], 0.0), s_pool, 1)
            w.do("gpsimd", lambda e: e.memset(identf[:], 0.0), s_pool, 1)
            w.wait("gpsimd", s_pool)
            w.do("gpsimd", lambda e: e.affine_select(
                out=ident_bf[:], in_=ident_bf[:], compare_op=ALU.not_equal,
                fill=1.0, base=0, pattern=[[-1, 128]], channel_multiplier=1), s_pool, 1)
            w.do("gpsimd", lambda e: e.affine_select(
                out=identf[:], in_=identf[:], compare_op=ALU.not_equal,
                fill=1.0, base=0, pattern=[[-1, 128]], channel_multiplier=1), s_pool, 1)
            w.do("sync", lambda e: e.dma_start(
                out=wg_sb[:].rearrange("p (dc e) -> p dc e", dc=DC),
                in_=Wg_t.ap().rearrange("(dc p) e -> p dc e", p=128)), s_pro, 16)
            for c in range(DC):
                for wsb, wt in ((wq_sb, Wq_t), (wk_sb, Wk_t), (wv_sb, Wv_t)):
                    w.do("gpsimd", lambda e, wsb=wsb, wt=wt, c=c: e.dma_start(
                        out=wsb[c][:], in_=wt.ap()[c * 128:(c + 1) * 128, :]), s_wslab, 16)
            cap["w_gw"] = w.v(s_wslab)

            w.do("vector", lambda e: e.memset(ones_bf[:], 1.0))
            w.do("vector", lambda e: e.memset(eps_sb[:], EPS))
            w.wait("vector", s_pro, cap["pro_sy"])
            w.do("vector", lambda e: e.tensor_copy(out=bv_bc[:], in_=bv_bc_f[:]))

            bias_jobs = [(bq_t, bq_c, 0.125), (bk_t, bk_c, 1.0)]
            for bt, bcol, scale in bias_jobs:
                w.wait("sync", s_pe)
                w.do("sync", lambda e, bt=bt: e.dma_start(
                    out=biasr[0:8, :], in_=bt.ap().rearrange("o (c p) -> (o c) p", p=128)), s_bias, 16)
                w.wait("tensor", s_bias)
                w.wait("tensor", s_pool, 4)
                w.wait("tensor", s_act)
                w.do("tensor", lambda e: e.transpose(ps_f[:, 0:8], biasr[0:8, :], identf[0:8, 0:8]), s_pe, 1)
                w.do("scalar", lambda e, bcol=bcol, scale=scale: (
                    e.wait_ge(s_pe, w.v(s_pe)),
                    e.activation(out=bcol[:], in_=ps_f[:, 0:8], func=AF.Identity,
                                 bias=0.0, scale=scale))[-1], s_act, 1)
            for j in range(3):
                w.wait("sync", s_pe)
                w.do("sync", lambda e, j=j: e.dma_start(
                    out=biasr[:, :],
                    in_=bup_t.ap().rearrange("e (c p) -> (e c) p", p=128)[j * 128:(j + 1) * 128, :]),
                    s_bias, 16)
                w.wait("tensor", s_bias)
                w.wait("tensor", s_act)
                w.do("tensor", lambda e: e.transpose(ps_f[:, 0:128], biasr[:, :], identf[:]), s_pe, 1)
                w.do("scalar", lambda e, j=j: (
                    e.wait_ge(s_pe, w.v(s_pe)),
                    e.activation(out=bup_c[:, j * 128:(j + 1) * 128], in_=ps_f[:, 0:128],
                                 func=AF.Identity, bias=0.0, scale=1.0))[-1], s_act, 1)
            w.wait("sync", s_pe)
            w.do("sync", lambda e: e.dma_start(
                out=biasr[0:96, :], in_=bdn_t.ap().rearrange("e (c p) -> (e c) p", p=128)), s_bias, 16)
            w.wait("tensor", s_bias)
            w.wait("tensor", s_act)
            w.do("tensor", lambda e: e.transpose(ps_f[:, 0:96], biasr[0:96, :], identf[0:96, 0:96]), s_pe, 1)
            w.do("scalar", lambda e: (
                e.wait_ge(s_pe, w.v(s_pe)),
                e.activation(out=bdn_c[:], in_=ps_f[:, 0:96], func=AF.Identity,
                             bias=0.0, scale=1.0))[-1], s_act, 1)

            tp_evac = []
            xt_read = {}

            def do_group(g, src_is_own, kjob, vjob, qjob):
                par = g % 2
                w.wait("gpsimd", s_pe, xt_read.get(par, 0))
                for j in range(4):
                    if src_is_own:
                        src = x_own[(g - 8) * 4 + j]
                    else:
                        src = x_ctx[(g * 4 + j) * 128:(g * 4 + j + 1) * 128, :]
                    w.do("gpsimd", lambda e, j=j, src=src: e.dma_start(
                        out=xt[par][j][:], in_=src), s_gx[par], 16)
                gx_done = w.v(s_gx[par])
                w.wait("tensor", s_gx[par], gx_done)
                ti = 0
                for j in range(4):
                    for c in range(DC):
                        bnk = ti % 2
                        col = (ti // 2) % 4
                        if len(tp_evac) >= 2:
                            w.wait("tensor", s_act, tp_evac[len(tp_evac) - 2])
                        w.do("tensor", lambda e, j=j, c=c, bnk=bnk, col=col: e.transpose(
                            ps_t[bnk][:, col * 128:(col + 1) * 128],
                            xt[par][j][:, c * 128:(c + 1) * 128], ident_bf[:]), s_pe, 1)
                        pv = w.v(s_pe)
                        w.do("scalar", lambda e, j=j, c=c, bnk=bnk, col=col, pv=pv: (
                            e.wait_ge(s_pe, pv),
                            e.activation(out=Xg[par][c][:, j * 128:(j + 1) * 128],
                                         in_=ps_t[bnk][:, col * 128:(col + 1) * 128],
                                         func=AF.Identity, bias=0.0, scale=1.0))[-1], s_act, 1)
                        tp_evac.append(w.v(s_act))
                        ti += 1
                xg_ready = w.v(s_act)
                xt_read[par] = w.v(s_pe)

                if kjob:
                    wsb, scr, bcol, wsem, scale = kjob
                    for dout in range(DC):
                        bank = dout % 2
                        w.wait("tensor", s_act, xg_ready)
                        w.wait("tensor", s_act)
                        for dc in range(DC):
                            w.do("tensor", lambda e, dc=dc, dout=dout, bank=bank, wsb=wsb: e.matmul(
                                ps_f[:, bank * 512:(bank + 1) * 512],
                                wsb[dc][:, dout * 128:(dout + 1) * 128],
                                Xg[par][dc][:], start=(dc == 0), stop=(dc == DC - 1)),
                                s_pe if dc == DC - 1 else None, 1)
                        pv = w.v(s_pe)
                        spar = dout % 2
                        w.wait("scalar", s_kst[spar], cap.get(("ksta", spar), 0))
                        w.do("scalar", lambda e, dout=dout, bank=bank, spar=spar, scale=scale, bcol=bcol, pv=pv: (
                            e.wait_ge(s_pe, pv),
                            e.activation(out=Ksta[spar][:], in_=ps_f[:, bank * 512:(bank + 1) * 512],
                                         func=AF.Identity, bias=bcol[:, dout:dout + 1],
                                         scale=scale))[-1], s_act, 1)
                        av = w.v(s_act)
                        w.wait("sync", s_act, av)
                        w.do("sync", lambda e, dout=dout, spar=spar, scr=scr: e.dma_start(
                            out=scr.ap()[dout][:, g * 512:g * 512 + 512],
                            in_=Ksta[spar][:]), s_kst[spar], 16)
                        cap[("ksta", spar)] = w.v(s_kst[spar])
                if vjob:
                    for ksub in range(4):
                        for half in range(2):
                            for dc in range(DC):
                                for d4 in range(4):
                                    dout = half * 4 + d4
                                    bank = 2 + d4
                                    if dc == 0:
                                        w.wait("tensor", s_act, xg_ready)
                                        w.wait("tensor", s_dve)
                                    w.do("tensor", lambda e, dc=dc, ksub=ksub, dout=dout, bank=bank: e.matmul(
                                        ps_f[:, bank * 512:bank * 512 + 128],
                                        Xg[par][dc][:, ksub * 128:(ksub + 1) * 128],
                                        wv_sb[dc][:, dout * 128:(dout + 1) * 128],
                                        start=(dc == 0), stop=(dc == DC - 1)),
                                        s_pe if dc == DC - 1 and d4 == 3 else None, 1)
                            pv = w.v(s_pe)
                            for d4 in range(4):
                                dout = half * 4 + d4
                                bank = 2 + d4
                                spar = d4 % 2
                                w.wait("scalar", s_dve, cap.get(("vsta", spar), 0))
                                w.do("scalar", lambda e, bank=bank, spar=spar, d4=d4, pv=pv: (
                                    e.wait_ge(s_pe, pv),
                                    e.activation(out=Vsta[spar][:, 0:128],
                                                 in_=ps_f[:, bank * 512:bank * 512 + 128],
                                                 func=AF.Identity, bias=0.0, scale=1.0))[-1], s_act, 1)
                                av = w.v(s_act)
                                w.wait("vector", s_vst[spar], cap.get(("vstb", spar), 0))
                                w.do("vector", lambda e, spar=spar, dout=dout, av=av: (
                                    e.wait_ge(s_act, av),
                                    e.tensor_tensor(out=Vstb[spar][:, 0:128], in0=Vsta[spar][:, 0:128],
                                                    in1=bv_bc[:, dout * 128:(dout + 1) * 128],
                                                    op=ALU.add))[-1], s_dve, 1)
                                cap[("vsta", spar)] = w.v(s_dve)
                                dv = w.v(s_dve)
                                w.wait("sync", s_dve, dv)
                                w.do("sync", lambda e, dout=dout, spar=spar, ksub=ksub: e.dma_start(
                                    out=Vscr.ap()[dout][g * 512 + ksub * 128:g * 512 + (ksub + 1) * 128, :],
                                    in_=Vstb[spar][:, 0:128]), s_vst[spar], 16)
                                cap[("vstb", spar)] = w.v(s_vst[spar])
                if qjob:
                    wsb, scr, bcol, wsem, scale = qjob
                    for dout in range(DC):
                        bank = dout % 2
                        w.wait("tensor", s_act, xg_ready)
                        w.wait("tensor", s_act)
                        for dc in range(DC):
                            w.do("tensor", lambda e, dc=dc, dout=dout, bank=bank, wsb=wsb: e.matmul(
                                ps_f[:, bank * 512:(bank + 1) * 512],
                                wsb[dc][:, dout * 128:(dout + 1) * 128],
                                Xg[par][dc][:], start=(dc == 0), stop=(dc == DC - 1)),
                                s_pe if dc == DC - 1 else None, 1)
                        pv = w.v(s_pe)
                        spar = dout % 2
                        w.wait("scalar", s_kst[spar], cap.get(("ksta", spar), 0))
                        w.do("scalar", lambda e, dout=dout, bank=bank, spar=spar, scale=scale, bcol=bcol, pv=pv: (
                            e.wait_ge(s_pe, pv),
                            e.activation(out=Ksta[spar][:], in_=ps_f[:, bank * 512:(bank + 1) * 512],
                                         func=AF.Identity, bias=bcol[:, dout:dout + 1],
                                         scale=scale))[-1], s_act, 1)
                        av = w.v(s_act)
                        w.wait("sync", s_act, av)
                        w.do("sync", lambda e, dout=dout, spar=spar, scr=scr: e.dma_start(
                            out=scr.ap()[dout][:, (g - 8) * 512:(g - 8) * 512 + 512],
                            in_=Ksta[spar][:]), s_kst[spar], 16)
                        cap[("ksta", spar)] = w.v(s_kst[spar])

            w.wait("tensor", s_wslab, cap["w_gw"])
            _g1 = int(os.environ.get("MK_P1GROUPS", "8"))
            _q1 = int(os.environ.get("MK_P1Q", "1"))
            for g in range(_g1):
                do_group(g, False,
                         kjob=(wk_sb, Kscr, bk_c, None, 1.0),
                         vjob=int(os.environ.get("MK_P1V", "1")) > 0, qjob=None)
            if _q1:
                for g in range(8, 12):
                    do_group(g, True, kjob=None, vjob=False,
                             qjob=(wq_sb, Qscr, bq_c, None, 0.125))
            for _s in (s_kst[0], s_kst[1], s_vst[0], s_vst[1]):
                w.wait("sync", _s)

        _ph = os.environ.get("MK_PHASES", "123")
        run_phase(phase1)
        p1.close()
        if "2" not in _ph:
            return nc

        p2 = ExitStack()
        bo_bc = p2.enter_context(nc.sbuf_tensor("bo_bc", [128, D], F32))
        wo_sb = [p2.enter_context(nc.sbuf_tensor(f"wo{c}", [128, D], BF16)) for c in range(DC)]
        q_sb = [p2.enter_context(nc.sbuf_tensor(f"q{p}", [128, NSLOT * 128], BF16)) for p in range(2)]
        k_sb = [p2.enter_context(nc.sbuf_tensor(f"k{p}", [128, NCTX], BF16)) for p in range(2)]
        v_sb = [p2.enter_context(nc.sbuf_tensor(f"v{p}", [128, NCTX], BF16)) for p in range(2)]
        band = [p2.enter_context(nc.sbuf_tensor(f"band{i}", [128, NCTX], BF16)) for i in range(4)]
        P_sb = [p2.enter_context(nc.sbuf_tensor(f"P{p}", [128, NCTX], BF16)) for p in range(2)]
        tmpA = [p2.enter_context(nc.sbuf_tensor(f"tmpA{i}", [128, 1024], BF16)) for i in range(2)]
        diag_sb = [p2.enter_context(nc.sbuf_tensor(f"diag{i}", [128, 512], BF16)) for i in range(2)]
        pt_sb = [p2.enter_context(nc.sbuf_tensor(f"pt{p}", [128, 512], BF16)) for p in range(2)]
        ao_sb = [[p2.enter_context(nc.sbuf_tensor(f"ao{p}_{c}", [128, 512], BF16))
                  for c in range(DC)] for p in range(2)]
        mgen = [p2.enter_context(nc.sbuf_tensor(f"mgen{i}", [128, 512], F32)) for i in range(3)]
        sp_sb = [p2.enter_context(nc.sbuf_tensor(f"sp{p}", [128, 4], F32)) for p in range(2)]
        nm_sb = [p2.enter_context(nc.sbuf_tensor(f"nm{p}", [128, 6], F32)) for p in range(2)]
        rr_sb = [p2.enter_context(nc.sbuf_tensor(f"rr{p}", [128, 6], F32)) for p in range(2)]
        xo_f = [p2.enter_context(nc.sbuf_tensor("xo0", [128, D], F32))]
        po_sb = p2.enter_context(nc.sbuf_tensor("po_sb", [128, D], F32))
        pt_tok = p2.enter_context(nc.sbuf_tensor("pt_tok", [128, D], F32))
        res_tok = p2.enter_context(nc.sbuf_tensor("res_tok", [128, D], F32))
        xnbf_stage = p2.enter_context(nc.sbuf_tensor("xnbf_stage", [128, D], BF16))
        xn_tok = p2.enter_context(nc.sbuf_tensor("xn_tok", [128, D], F32))
        xn_bf = p2.enter_context(nc.sbuf_tensor("xn_bf", [128, D], BF16))
        xn_fm = p2.enter_context(nc.sbuf_tensor("xn_fm", [128, D], F32))
        ln_s = p2.enter_context(nc.sbuf_tensor("ln_s", [128, 8], F32))
        ps_S = [p2.enter_context(nc.psum_tensor(f"ps_S{i}", [128, 2 * 512], F32)) for i in range(2)]
        ps_PT = [p2.enter_context(nc.psum_tensor(f"ps_PT{i}", [128, 512], F32)) for i in range(2)]
        ps_A = p2.enter_context(nc.psum_tensor("ps_A", [128, 1024], F32))

        def phase2(w):
            _nss = int(os.environ.get("MK_P2SS", str(NSS)))
            cap2 = {}
            bank_rd = [0, 0]
            aA_rd2 = [0, 0]
            pending = []
            diag_rd = [0, 0]

            def pop_steps(n):
                for _ in range(min(n, len(pending))):
                    pending.pop(0)[1]()

            def flush_steps():
                while pending:
                    pending.pop(0)[1]()

            def flush_older(min_keep):
                while pending and pending[0][0] < min_keep:
                    pending.pop(0)[1]()
            pt_evac = [0, 0]
            pt_evacd = [0, 0]
            pt_read = [0, 0]
            kvq_rd = [0, 0]
            p_read = [0, 0]
            ao_rd = [0, 0]
            nm_rd = [0, 0]
            rr_rd = [0, 0]
            xo_rd = [0, 0]
            aA_rd = 0
            unit_idx = 0

            w.do("sync", lambda e: e.dma_start(
                out=bo_bc[:], in_=bass.AP(bo_t, 0, [[0, 128], [1, D]])), s_bo, 16)
            cap2["bo"] = w.v(s_bo)
            for c in range(DC):
                w.do("gpsimd", lambda e, c=c: e.dma_start(
                    out=wo_sb[c][:], in_=Wo_t.ap()[c * 128:(c + 1) * 128, :]), s_wo, 16)
            cap2["wo"] = w.v(s_wo)

            for ss in range(_nss):
                kmax = 1024 * (ss + 1)
                for sl in range(4):
                    s = 4 * ss + sl
                    kl = klen(s)
                    for kc in range(cdiv(kl, 512)):
                        csz = min(512, kl - kc * 512)
                        w.wait("gpsimd", s_dve, cap2.get(("mg", 0), 0))
                        w.do("gpsimd", lambda e, kc=kc, csz=csz: e.iota(
                            mgen[0][:, 0:csz], [[-1, csz]], base=-(512 * kc),
                            channel_multiplier=1, allow_small_or_imprecise_dtypes=True), s_pool, 1)
                        pl = w.v(s_pool)
                        w.wait("vector", s_pool, pl)
                        w.do("vector", lambda e, s=s, csz=csz: e.tensor_scalar(
                            out=mgen[1][:, 0:csz], in0=mgen[0][:, 0:csz],
                            scalar1=rowoff_bc[:, s:s + 1], scalar2=1.0,
                            op0=ALU.add, op1=ALU.max), None)
                        w.do("vector", lambda e, s=s, csz=csz: e.tensor_scalar(
                            out=mgen[0][:, 0:csz], in0=mgen[0][:, 0:csz],
                            scalar1=rowoffn_bc[:, s:s + 1], scalar2=None,
                            op0=ALU.is_ge), s_dve, 1)
                        cap2[("mg", 0)] = w.v(s_dve)
                        dv = w.v(s_dve)
                        w.wait("scalar", s_dve, dv)
                        w.do("scalar", lambda e, csz=csz: e.activation(
                            out=mgen[2][:, 0:csz], in_=mgen[1][:, 0:csz], func=AF.Ln), None)
                        w.do("scalar", lambda e, csz=csz: e.activation(
                            out=mgen[1][:, 0:csz], in_=mgen[2][:, 0:csz], func=AF.Exp,
                            bias=0.0, scale=-0.25), s_act, 1)
                        av = w.v(s_act)
                        w.wait("vector", s_act, av)
                        w.do("vector", lambda e, sl=sl, kc=kc, csz=csz: e.tensor_tensor(
                            out=band[sl][:, kc * 512:kc * 512 + csz], in0=mgen[1][:, 0:csz],
                            in1=mgen[0][:, 0:csz], op=ALU.mult), s_dve, 1)
                        cap2[("mg", 0)] = w.v(s_dve)

                for hp in range(HPN):
                    par = hp % 2
                    w.wait("sync", s_pe, kvq_rd[par])
                    w.do("sync", lambda e, hp=hp, par=par, kmax=kmax: e.dma_start(
                        out=k_sb[par][:, 0:kmax], in_=Kscr.ap()[hp][:, 0:kmax]), s_kr[par], 16)
                    w.do("sync", lambda e, hp=hp, par=par, kmax=kmax: e.dma_start(
                        out=v_sb[par][:, 0:kmax].rearrange("p (kc v) -> p kc v", v=128),
                        in_=Vscr.ap()[hp][0:kmax, :].rearrange("(kc p) v -> p kc v", p=128)),
                        s_kr[par], 16)
                    w.do("sync", lambda e, hp=hp, par=par: e.dma_start(
                        out=q_sb[par][:], in_=Qscr.ap()[hp][:]), s_kr[par], 16)
                    kvq_ready = w.v(s_kr[par])

                    for h in range(2):
                        for sl in range(4):
                            s = 4 * ss + sl
                            kl = klen(s)
                            C = cdiv(kl, 512)
                            Cp = min(C, 5)
                            up2 = unit_idx % 2
                            hp0 = h * 64

                            R = cdiv(kl, 1024)
                            w.wait("vector", s_pe, p_read[up2])
                            w.wait("vector", s_act, nm_rd[up2])
                            for r in range(R):
                                klr = min(1024, kl - r * 1024)
                                for kc in range(cdiv(klr, 512)):
                                    csz = min(512, klr - kc * 512)
                                    w.wait("tensor", s_kr[par], kvq_ready)
                                    w.wait("tensor", s_act, bank_rd[up2])
                                    w.do("tensor", lambda e, r=r, kc=kc, csz=csz: e.matmul(
                                        ps_S[up2][:, kc * 512:kc * 512 + csz],
                                        q_sb[par][hp0:hp0 + 64, s * 128:(s + 1) * 128],
                                        k_sb[par][hp0:hp0 + 64,
                                                  r * 1024 + kc * 512:r * 1024 + kc * 512 + csz],
                                        start=True, stop=True),
                                        s_pe if kc == cdiv(klr, 512) - 1 else None, 1)
                                pv = w.v(s_pe)
                                w.do("vector", lambda e, klr=klr, r=r, pv=pv: (
                                    e.wait_ge(s_pe, pv),
                                    e.reduce_max(nm_sb[up2][:, r:r + 1], ps_S[up2][:, 0:klr],
                                                 axis=AX.X, negate=True))[-1], s_dve, 1)
                                nmv = w.v(s_dve)
                                tp = r % 2
                                w.wait("scalar", s_dve, max(nmv, cap2.get(("tmpA", up2, tp), 0)))
                                w.do("scalar", lambda e, klr=klr, r=r, tp=tp: e.activation(
                                    out=tmpA[tp][:, 0:klr], in_=ps_S[up2][:, 0:klr], func=AF.Exp,
                                    bias=nm_sb[up2][:, r:r + 1], scale=1.0), s_act, 1)
                                bank_rd[up2] = w.v(s_act)
                                av = w.v(s_act)
                                w.do("vector", lambda e, klr=klr, r=r, sl=sl, tp=tp, av=av: (
                                    e.wait_ge(s_act, av),
                                    e.scalar_tensor_tensor(
                                        out=P_sb[up2][:, r * 1024:r * 1024 + klr],
                                        in0=tmpA[tp][:, 0:klr], scalar=1.0,
                                        in1=band[sl][:, r * 1024:r * 1024 + klr],
                                        op0=ALU.mult, op1=ALU.mult,
                                        accum_out=sp_sb[up2][:, r:r + 1]))[-1], s_dve, 1)
                                cap2[("tmpA", up2, tp)] = w.v(s_dve)
                                pop_steps(3)
                            nm_rd[up2] = w.v(s_act)
                            w.wait("vector", s_dve)
                            if R > 1:
                                w.do("vector", lambda e, R=R: e.tensor_reduce(
                                    nm_sb[up2][:, 4:5], nm_sb[up2][:, 0:R], axis=AX.X,
                                    op=ALU.min), s_dve, 1)
                                w.wait("vector", s_dve)
                                w.do("vector", lambda e, R=R: e.tensor_scalar(
                                    out=rr_sb[up2][:, 0:R], in0=nm_sb[up2][:, 0:R],
                                    scalar1=nm_sb[up2][:, 4:5], scalar2=-1.0,
                                    op0=ALU.subtract, op1=ALU.mult), s_dve, 1)
                                dv = w.v(s_dve)
                                w.do("scalar", lambda e, R=R, dv=dv: (
                                    e.wait_ge(s_dve, dv),
                                    e.activation(out=rr_sb[up2][:, 0:R], in_=rr_sb[up2][:, 0:R],
                                                 func=AF.Exp))[-1], s_act, 1)
                                av = w.v(s_act)
                                w.wait("vector", s_act, av)
                                w.do("vector", lambda e, R=R: e.scalar_tensor_tensor(
                                    out=sp_sb[up2][:, 0:R], in0=sp_sb[up2][:, 0:R],
                                    scalar=1.0, in1=rr_sb[up2][:, 0:R],
                                    op0=ALU.mult, op1=ALU.mult), s_dve, 1)
                                w.wait("vector", s_dve)
                                w.do("vector", lambda e, R=R: e.reduce_sum(
                                    ln_s[:, 0:1], sp_sb[up2][:, 0:R], axis=AX.X), s_dve, 1)
                            else:
                                w.do("vector", lambda e: e.tensor_copy(
                                    out=ln_s[:, 0:1], in_=sp_sb[up2][:, 0:1]), s_dve, 1)
                            w.wait("vector", s_dve)
                            w.do("vector", lambda e: e.reciprocal(
                                rr_sb[up2][:, 5:6], ln_s[:, 0:1]), s_dve, 1)
                            w.wait("vector", s_dve)
                            w.wait("vector", s_pe, diag_rd[up2])
                            if R > 1:
                                w.do("vector", lambda e, R=R: e.tensor_scalar(
                                    out=rr_sb[up2][:, 0:R], in0=rr_sb[up2][:, 0:R],
                                    scalar1=rr_sb[up2][:, 5:6], scalar2=None,
                                    op0=ALU.mult), s_dve, 1)
                                w.wait("vector", s_dve)
                                for r in range(R):
                                    w.do("vector", lambda e, r=r: e.tensor_scalar(
                                        out=diag_sb[up2][:, r * 128:(r + 1) * 128],
                                        in0=ident_bf[:], scalar1=rr_sb[up2][:, r:r + 1],
                                        scalar2=None, op0=ALU.mult),
                                        s_dve if r == R - 1 else None, 1)
                            else:
                                w.do("vector", lambda e: e.tensor_scalar(
                                    out=diag_sb[up2][:, 0:128], in0=ident_bf[:],
                                    scalar1=rr_sb[up2][:, 5:6], scalar2=None,
                                    op0=ALU.mult), s_dve, 1)
                            rr_rd[up2] = w.v(s_dve)
                            plv = w.v(s_dve)

                            def queue_consume(up2=up2, par=par, hp=hp, hp0=hp0,
                                              sl=sl, ss=ss, kl=kl, plv=plv):
                                nt = kl // 128
                                ngr = cdiv(nt, 4)

                                def do_transposes(tg):
                                    half = tg % 2
                                    cnt = min(4, nt - tg * 4)
                                    if tg == 0:
                                        w.wait("tensor", s_dve, plv)
                                    w.wait("tensor", s_act, pt_evac[half])
                                    for j in range(cnt):
                                        tc = tg * 4 + j
                                        r = tc // 8
                                        w.do("tensor", lambda e, half=half, j=j, tc=tc, r=r: e.matmul(
                                            ps_PT[half][:, j * 128:(j + 1) * 128],
                                            P_sb[up2][:, tc * 128:(tc + 1) * 128],
                                            diag_sb[up2][:, r * 128:(r + 1) * 128],
                                            start=True, stop=True),
                                            s_pe, 1)
                                    diag_rd[up2] = w.v(s_pe)
                                    pv = w.v(s_pe)
                                    w.do("scalar", lambda e, half=half, cnt=cnt, pv=pv,
                                         prd=pt_read[half]: (
                                        e.wait_ge(s_pe, max(pv, prd)),
                                        e.activation(out=pt_sb[half][:, 0:cnt * 128],
                                                     in_=ps_PT[half][:, 0:cnt * 128],
                                                     func=AF.Identity, bias=0.0, scale=1.0))[-1],
                                        s_act, 1)
                                    pt_evacd[half] = w.v(s_act)

                                def do_vmms(tg):
                                    half = tg % 2
                                    cnt = min(4, nt - tg * 4)
                                    w.wait("tensor", s_act, pt_evacd[half])
                                    if tg == 0:
                                        w.wait("tensor", s_act, max(aA_rd2[up2], aA_rd))
                                    for j in range(cnt):
                                        tc = tg * 4 + j
                                        w.do("tensor", lambda e, half=half, j=j, tc=tc: e.matmul(
                                            ps_A[0:64, up2 * 512:up2 * 512 + 128],
                                            v_sb[par][:, tc * 128 + hp0:tc * 128 + hp0 + 64],
                                            pt_sb[half][:, j * 128:(j + 1) * 128],
                                            start=(tc == 0), stop=(tc == nt - 1)),
                                            s_pe if tc == nt - 1 else None, 1)
                                    pt_read[half] = w.v(s_pe)
                                    pt_evac[half] = pt_evacd[half]
                                    if tg == ngr - 1:
                                        p_read[up2] = w.v(s_pe)
                                        kvq_rd[par] = w.v(s_pe)
                                        pv2 = w.v(s_pe)
                                        w.wait("scalar", s_pe, pv2)
                                        sspar = ss % 2
                                        w.wait("scalar", s_pe, ao_rd[sspar])
                                        w.do("scalar", lambda e: e.activation(
                                            out=ao_sb[sspar][hp][hp0:hp0 + 64,
                                                                 sl * 128:(sl + 1) * 128],
                                            in_=ps_A[0:64, up2 * 512:up2 * 512 + 128],
                                            func=AF.Identity, bias=0.0, scale=1.0),
                                            s_act, 1)
                                        aA_rd2[up2] = w.v(s_act)

                                def mk_step(k):
                                    def step():
                                        if k < ngr:
                                            do_transposes(k)
                                        if k >= 1:
                                            do_vmms(k - 1)
                                    return step

                                for k in range(ngr + 1):
                                    pending.append((unit_idx, mk_step(k)))

                            flush_steps()
                            queue_consume()
                            unit_idx += 1

                flush_steps()
                sspar = ss % 2
                ao_ready = w.v(s_act)
                if _dbg2 and ss == 0:
                    w.wait("sync", s_act, ao_ready)
                    for c in range(DC):
                        w.do("sync", lambda e, c=c: e.dma_start(
                            out=AOdbg.ap()[c], in_=ao_sb[sspar][c][:]), s_pro, 16)
                    for pp in range(2):
                        w.do("sync", lambda e, pp=pp: e.dma_start(
                            out=Pdbg.ap()[pp], in_=P_sb[pp][:]), s_pro, 16)
                w.wait("tensor", s_wo, cap2["wo"])
                for sl in range(4):
                    qt = 4 * ss + sl
                    xop = 0
                    w.wait("sync", s_dve, xo_rd[xop])
                    w.do("sync", lambda e, qt=qt, xop=xop: e.dma_start(
                        out=xo_f[xop][:], in_=x_own[qt]), s_xo, 16)
                    xo_ready = w.v(s_xo)
                    for dout in range(DC):
                        w.wait("tensor", s_act, ao_ready)
                        w.wait("tensor", s_act)
                        for dc in range(DC):
                            w.do("tensor", lambda e, dc=dc, dout=dout, sl=sl: e.matmul(
                                ps_A[:, 128:256],
                                wo_sb[dc][:, dout * 128:(dout + 1) * 128],
                                ao_sb[sspar][dc][:, sl * 128:(sl + 1) * 128],
                                start=(dc == 0), stop=(dc == DC - 1)),
                                s_pe if dc == DC - 1 else None, 1)
                        pv = w.v(s_pe)
                        w.do("scalar", lambda e, dout=dout, pv=pv: (
                            e.wait_ge(s_pe, pv),
                            e.activation(out=po_sb[:, dout * 128:(dout + 1) * 128],
                                         in_=ps_A[:, 128:256], func=AF.Identity,
                                         bias=0.0, scale=1.0))[-1], s_act, 1)
                    ao_rd[sspar] = w.v(s_pe)
                    for dout in range(DC):
                        w.wait("tensor", s_act)
                        w.do("tensor", lambda e, dout=dout: e.transpose(
                            ps_A[:, 256:384],
                            po_sb[:, dout * 128:(dout + 1) * 128], identf[:]), s_pe, 1)
                        pv = w.v(s_pe)
                        w.do("scalar", lambda e, dout=dout, pv=pv: (
                            e.wait_ge(s_pe, pv),
                            e.activation(out=pt_tok[:, dout * 128:(dout + 1) * 128],
                                         in_=ps_A[:, 256:384], func=AF.Identity,
                                         bias=0.0, scale=1.0))[-1], s_act, 1)
                    av = w.v(s_act)
                    if _dbg2 and ss == 0:
                        w.wait("sync", s_act, av)
                        w.do("sync", lambda e, sl=sl: e.dma_start(
                            out=POdbg.ap()[sl], in_=pt_tok[:]), s_pro, 16)
                    w.wait("vector", s_act, av)
                    w.wait("vector", s_xo, xo_ready)
                    w.wait("vector", s_bo, cap2["bo"])
                    w.do("vector", lambda e, xop=xop: e.tensor_tensor(
                        out=res_tok[:], in0=pt_tok[:], in1=xo_f[xop][:], op=ALU.add), None)
                    w.do("vector", lambda e: e.tensor_tensor(
                        out=res_tok[:], in0=res_tok[:], in1=bo_bc[:], op=ALU.add), s_dve, 1)
                    if _dbg2 and ss == 0:
                        w.wait("sync", s_dve)
                        w.do("sync", lambda e, sl=sl: e.dma_start(
                            out=REdbg.ap()[sl], in_=res_tok[:]), s_pro, 16)
                    xo_rd[xop] = w.v(s_dve) + 1
                    w.do("vector", lambda e: e.reduce_sum(
                        ln_s[:, 0:1], res_tok[:], axis=AX.X), s_dve, 1)
                    dv = w.v(s_dve)
                    w.do("scalar", lambda e, dv=dv: (
                        e.wait_ge(s_dve, dv),
                        e.activation(out=po_sb[:], in_=res_tok[:], func=AF.Square,
                                     accum_out=ln_s[:, 1:2]))[-1], s_act, 1)
                    av = w.v(s_act)
                    w.wait("vector", s_act, av)
                    w.wait("vector", s_dve)
                    w.do("vector", lambda e: e.tensor_scalar(
                        out=ln_s[:, 2:3], in0=ln_s[:, 0:1], scalar1=1.0 / D,
                        scalar2=None, op0=ALU.mult), s_dve, 1)
                    w.do("vector", lambda e: e.tensor_scalar(
                        out=ln_s[:, 3:4], in0=ln_s[:, 1:2], scalar1=1.0 / D,
                        scalar2=None, op0=ALU.mult), s_dve, 1)
                    w.wait("vector", s_dve)
                    w.do("vector", lambda e: e.tensor_tensor(
                        out=ln_s[:, 4:5], in0=ln_s[:, 2:3], in1=ln_s[:, 2:3], op=ALU.mult), s_dve, 1)
                    w.wait("vector", s_dve)
                    w.do("vector", lambda e: e.tensor_tensor(
                        out=ln_s[:, 5:6], in0=ln_s[:, 3:4], in1=ln_s[:, 4:5], op=ALU.subtract), s_dve, 1)
                    dv = w.v(s_dve)
                    w.do("scalar", lambda e, dv=dv: (
                        e.wait_ge(s_dve, dv),
                        e.activation(out=ln_s[:, 6:7], in_=ln_s[:, 5:6], func=AF.Ln,
                                     bias=eps_sb[:], scale=1.0))[-1], None)
                    w.do("scalar", lambda e: e.activation(
                        out=ln_s[:, 7:8], in_=ln_s[:, 6:7], func=AF.Exp, bias=0.0,
                        scale=-0.5), s_act, 1)
                    av = w.v(s_act)
                    w.wait("vector", s_act, av)
                    w.do("vector", lambda e: e.tensor_scalar(
                        out=xn_tok[:], in0=res_tok[:], scalar1=ln_s[:, 2:3],
                        scalar2=ln_s[:, 7:8], op0=ALU.subtract, op1=ALU.mult), s_dve, 1)
                    dv = w.v(s_dve)
                    w.do("scalar", lambda e, dv=dv: (
                        e.wait_ge(s_dve, dv),
                        e.activation(out=xn_bf[:], in_=xn_tok[:], func=AF.Identity,
                                     bias=0.0, scale=1.0))[-1], s_act, 1)
                    av = w.v(s_act)
                    w.wait("scalar", s_xnb, cap2.get("xnbf", 0))
                    for c in range(DC):
                        half = c % 2
                        w.wait("tensor", s_act, max(av, pt_evac[half]))
                        w.do("tensor", lambda e, c=c, half=half: e.matmul(
                            ps_PT[half][:, 0:128],
                            xn_bf[:, c * 128:(c + 1) * 128], ident_bf[:],
                            start=True, stop=True), s_pe, 1)
                        pv = w.v(s_pe)
                        w.do("scalar", lambda e, c=c, qt=qt, half=half, pv=pv: (
                            e.wait_ge(s_pe, pv),
                            e.activation(out=xnbf_stage[:, c * 128:(c + 1) * 128],
                                         in_=ps_PT[half][:, 0:128],
                                         func=AF.Identity, bias=0.0, scale=1.0))[-1], s_act, 1)
                        pt_evac[half] = w.v(s_act)
                    av2 = w.v(s_act)
                    w.wait("sync", s_act, av2)
                    w.do("sync", lambda e, qt=qt: e.dma_start(
                        out=XNBFscr.ap()[qt].rearrange("c p t -> p c t"),
                        in_=xnbf_stage[:].rearrange("p (c t) -> p c t", c=DC)), s_xnb, 16)
                    cap2["xnbf"] = w.v(s_xnb)
                    w.wait("scalar", s_xnf, cap2.get("xnfm", 0))
                    for c in range(DC):
                        w.wait("tensor", s_act)
                        w.do("tensor", lambda e, c=c: e.transpose(
                            ps_A[:, 256:384], xn_tok[:, c * 128:(c + 1) * 128], identf[:]),
                            s_pe, 1)
                        pv = w.v(s_pe)
                        w.do("scalar", lambda e, c=c, pv=pv: (
                            e.wait_ge(s_pe, pv),
                            e.activation(out=xn_fm[:, c * 128:(c + 1) * 128],
                                         in_=ps_A[:, 256:384], func=AF.Identity,
                                         bias=0.0, scale=1.0))[-1], s_act, 1)
                    av = w.v(s_act)
                    w.wait("sync", s_act, av)
                    w.do("sync", lambda e, qt=qt: e.dma_start(
                        out=XNscr.ap()[qt].rearrange("c p t -> p c t"),
                        in_=xn_fm[:].rearrange("p (c t) -> p c t", c=DC)), s_xnf, 16)
                    cap2["xnfm"] = w.v(s_xnf)
                aA_rd = w.v(s_act)
            w.wait("sync", s_xnf)
            w.wait("sync", s_xnb)

        run_phase(phase2)
        p2.close()
        if "3" not in _ph:
            return nc

        p3 = ExitStack()
        xnT = [p3.enter_context(nc.sbuf_tensor(f"xnT{c}", [128, NSLOT * 128], BF16))
               for c in range(DC)]
        acc = [p3.enter_context(nc.sbuf_tensor(f"acc{c}", [128, 1024], F32)) for c in range(DC)]
        hT = [p3.enter_context(nc.sbuf_tensor(f"hT{c}", [128, 1024], BF16)) for c in range(HCN)]
        up_sb = [p3.enter_context(nc.sbuf_tensor(f"up{p}", [128, 4096], BF16)) for p in range(3)]
        dn_sb = [p3.enter_context(nc.sbuf_tensor(f"dn{p}", [128, 4096], BF16)) for p in range(2)]
        cwT = p3.enter_context(nc.sbuf_tensor("cwT", [16, 2048], BF16))
        cwrow = [p3.enter_context(nc.sbuf_tensor(f"cwrow{i}", [1, 1024], BF16)) for i in range(2)]
        cwbc = [p3.enter_context(nc.sbuf_tensor(f"cwbc{p}", [128, 1024], F32)) for p in range(2)]
        ysb = [p3.enter_context(nc.sbuf_tensor(f"ysb{p}", [128, 512], F32)) for p in range(2)]
        g_sb = [p3.enter_context(nc.sbuf_tensor(f"g{j}", [128, E], F32)) for j in range(4)]
        gm = p3.enter_context(nc.sbuf_tensor("gm", [128, 8], F32))
        cw_f = p3.enter_context(nc.sbuf_tensor("cw_f", [128, E], F32))
        r2_tok = p3.enter_context(nc.sbuf_tensor("r2_tok", [128, D], F32))
        o_tok = [p3.enter_context(nc.sbuf_tensor(f"otok{p}", [128, D], F32)) for p in range(2)]
        ln3 = p3.enter_context(nc.sbuf_tensor("ln3", [128, 8], F32))
        PS8 = p3.enter_context(nc.psum_tensor("PS8", [128, 8 * 512], F32))

        def phase3(w):
            cap3 = {}
            for c in range(DC):
                w.do("sync", lambda e, c=c: e.dma_start(
                    out=xnT[c][:].rearrange("p (qt t) -> p qt t", t=128),
                    in_=XNBFscr.ap()[:, c].rearrange("qt p t -> p qt t")), s_xnt, 16)
            xnT_ready = w.v(s_xnt)
            w.wait("tensor", s_xnt, xnT_ready)
            w.do("gpsimd", lambda e: e.memset(cwT[:], 0.0), s_pool, 1)
            hT_rd = [0] * HCN
            acc_wr_last = [0] * DC
            up_rd = [0, 0, 0]
            dn_rd = [0, 0]
            ysb_rd = [0, 0]
            cwbc_rd = [0, 0]
            acc_pe_rd = 0
            otok_rd = [0, 0]
            upbank_rd = [0] * 6
            ybank_rd = [0] * 8

            for tile in range(2):
                t0 = tile * 1024
                if tile == 1:
                    w.wait("sync", s_ws)
                w.wait("sync", s_pe, acc_pe_rd)
                w.wait("sync", s_dve, cap3.get("acc_dve", 0))
                for c in range(DC):
                    w.do("sync", lambda e, c=c, tile=tile: e.dma_start(
                        out=acc[c][:].rearrange("p (qt t) -> p qt t", t=128),
                        in_=XNscr.ap()[tile * 8:(tile + 1) * 8, c].rearrange("qt p t -> p qt t")),
                        s_acc, 16)
                acc_ready = w.v(s_acc)

                for sl8 in range(8):
                    qt = tile * 8 + sl8
                    if sl8 == 0:
                        w.wait("scalar", s_pool)
                        w.wait("tensor", s_act)
                        w.wait("tensor", s_acc, acc_ready)
                    w.wait("tensor", s_act, cap3.get("gate_act", 0))
                    for dc in range(DC):
                        w.do("tensor", lambda e, dc=dc, sl8=sl8: e.matmul(
                            PS8[:, 0:E], acc[dc][:, sl8 * 128:(sl8 + 1) * 128],
                            wg_sb[:, dc * E:(dc + 1) * E],
                            start=(dc == 0), stop=(dc == DC - 1)),
                            s_pe if dc == DC - 1 else None, 1)
                    pv = w.v(s_pe)
                    w.wait("vector", s_pe, pv)
                    def gstep(fn):
                        w.wait("vector", s_dve)
                        w.do("vector", fn, s_dve, 1)
                    gstep(lambda e: e.reduce_max(gm[:, 0:1], PS8[:, 0:E], axis=AX.X))
                    gstep(lambda e: e.tensor_scalar(
                        out=g_sb[0][:], in0=PS8[:, 0:E], scalar1=gm[:, 0:1], scalar2=None,
                        op0=ALU.is_ge))
                    gstep(lambda e: e.scalar_tensor_tensor(
                        out=g_sb[1][:], in0=g_sb[0][:], scalar=NEG, in1=PS8[:, 0:E],
                        op0=ALU.mult, op1=ALU.add))
                    gstep(lambda e: e.reduce_max(gm[:, 1:2], g_sb[1][:], axis=AX.X))
                    gstep(lambda e: e.tensor_scalar(
                        out=g_sb[2][:], in0=g_sb[1][:], scalar1=gm[:, 1:2], scalar2=None,
                        op0=ALU.is_ge))
                    gstep(lambda e: e.scalar_tensor_tensor(
                        out=g_sb[3][:], in0=g_sb[2][:], scalar=NEG, in1=g_sb[1][:],
                        op0=ALU.mult, op1=ALU.add))
                    gstep(lambda e: e.reduce_max(gm[:, 2:3], g_sb[3][:], axis=AX.X))
                    gstep(lambda e: e.tensor_scalar(
                        out=gm[:, 3:4], in0=gm[:, 0:1], scalar1=-1.0, scalar2=None,
                        op0=ALU.mult))
                    gstep(lambda e: e.tensor_scalar(
                        out=g_sb[0][:], in0=PS8[:, 0:E], scalar1=gm[:, 2:3], scalar2=None,
                        op0=ALU.is_ge))
                    dv = w.v(s_dve)
                    w.do("scalar", lambda e, dv=dv: (
                        e.wait_ge(s_dve, dv),
                        e.activation(out=g_sb[1][:], in_=PS8[:, 0:E], func=AF.Exp,
                                     bias=gm[:, 3:4], scale=1.0))[-1], s_act, 1)
                    av = w.v(s_act)
                    w.wait("vector", s_act, av)
                    gstep(lambda e: e.tensor_tensor(
                        out=g_sb[2][:], in0=g_sb[1][:], in1=g_sb[0][:], op=ALU.mult))
                    gstep(lambda e: e.reduce_sum(gm[:, 4:5], g_sb[2][:], axis=AX.X))
                    gstep(lambda e: e.reciprocal(gm[:, 5:6], gm[:, 4:5]))
                    gstep(lambda e: e.tensor_scalar(
                        out=cw_f[:], in0=g_sb[2][:], scalar1=gm[:, 5:6], scalar2=None,
                        op0=ALU.mult))
                    w.wait("vector", s_dve)
                    w.do("vector", lambda e: e.engine_nop(), s_dve, 1)
                    dv = w.v(s_dve)
                    w.wait("tensor", s_dve, dv)
                    w.wait("tensor", s_act)
                    w.do("tensor", lambda e: e.transpose(
                        PS8[0:E, 512:640], cw_f[:], identf[:]), s_pe, 1)
                    pv = w.v(s_pe)
                    w.do("scalar", lambda e, qt=qt, pv=pv: (
                        e.wait_ge(s_pe, pv),
                        e.activation(out=cwT[0:E, qt * 128:(qt + 1) * 128],
                                     in_=PS8[0:E, 512:640], func=AF.Identity,
                                     bias=0.0, scale=1.0))[-1], s_act, 1)
                    cap3["gate_act"] = w.v(s_act)

                for e_i in range(int(os.environ.get("MK_P3E", str(E)))):
                    epar = e_i % 2
                    if e_i == 0:
                        w.wait("sync", s_act, cap3.get("gate_act", 0))
                        w.wait("sync", s_pe, cap3.get(("cwrow_pe", 0), 0))
                        w.do("sync", lambda e, tile=tile: e.dma_start(
                            out=cwrow[0][:], in_=cwT[0:1, tile * 1024:(tile + 1) * 1024]),
                            s_cw, 16)
                        cap3[("cwr", 0)] = w.v(s_cw)
                    if e_i + 1 < E:
                        npar = (e_i + 1) % 2
                        w.wait("sync", s_pe, cap3.get(("cwrow_pe", npar), 0))
                        w.do("sync", lambda e, e_i=e_i, tile=tile, npar=npar: e.dma_start(
                            out=cwrow[npar][:],
                            in_=cwT[e_i + 1:e_i + 2, tile * 1024:(tile + 1) * 1024]),
                            s_cw, 16)
                        cap3[("cwr", npar)] = w.v(s_cw)
                    w.wait("tensor", s_cw, cap3[("cwr", epar)])
                    w.wait("tensor", s_dve, cwbc_rd[epar])
                    w.wait("tensor", s_act, max(ybank_rd[6], ybank_rd[7]))
                    for j in range(2):
                        w.do("tensor", lambda e, j=j, epar=epar: e.matmul(
                            PS8[:, 6 * 512 + j * 512:6 * 512 + (j + 1) * 512],
                            ones_bf[:], cwrow[epar][0:1, j * 512:(j + 1) * 512],
                            start=True, stop=True), s_pe, 1)
                    cap3[("cwrow_pe", epar)] = w.v(s_pe)
                    pv = w.v(s_pe)
                    w.wait("scalar", s_dve, cwbc_rd[epar])
                    w.do("scalar", lambda e, epar=epar, pv=pv: (
                        e.wait_ge(s_pe, pv),
                        e.activation(out=cwbc[epar][:], in_=PS8[:, 6 * 512:8 * 512],
                                     func=AF.Identity, bias=0.0, scale=1.0))[-1], s_act, 1)
                    ybank_rd[6] = w.v(s_act)
                    ybank_rd[7] = w.v(s_act)
                    cwbc_ready = w.v(s_act)

                    for hcg in range(8):
                        spar = hcg % 3
                        w.wait("gpsimd" if tile == 0 else "sync", s_pe, up_rd[spar])
                        if tile == 0:
                            w.wait("gpsimd", s_ws, cap3.get(("upw", spar), 0))
                            w.do("gpsimd", lambda e, e_i=e_i, hcg=hcg, spar=spar: e.dma_start(
                                out=up_sb[spar][:].rearrange("p (dc hch) -> p dc hch", dc=DC),
                                in_=Wup_t.ap()[e_i].rearrange(
                                    "(dc p) (hcg hch) -> hcg p dc hch",
                                    p=128, hch=512)[hcg]), s_upg[spar], 16)
                            slab_ready = w.v(s_upg[spar])
                            w.wait("sync", s_upg[spar], slab_ready)
                            w.do("sync", lambda e, e_i=e_i, hcg=hcg, spar=spar: e.dma_start(
                                out=WUPscr.ap()[e_i, hcg], in_=up_sb[spar][:]), s_ws, 16)
                            cap3[("upw", spar)] = w.v(s_ws)
                            slab_sem, slab_val = s_upg[spar], slab_ready
                        else:
                            w.do("sync", lambda e, e_i=e_i, hcg=hcg, spar=spar: e.dma_start(
                                out=up_sb[spar][:], in_=WUPscr.ap()[e_i, hcg]), s_upr[spar], 16)
                            slab_sem, slab_val = s_upr[spar], w.v(s_upr[spar])
                        for hc in range(4):
                            hchunk = hcg * 4 + hc
                            for ts_ in range(2):
                                bank = (hchunk * 2 + ts_) % 6
                                w.wait("tensor", slab_sem, slab_val)
                                w.wait("tensor", s_act, upbank_rd[bank])
                                w.wait("scalar", s_pe, hT_rd[hchunk])
                                for dc in range(DC):
                                    w.do("tensor", lambda e, dc=dc, hc=hc, ts_=ts_, bank=bank, spar=spar, t0=t0: e.matmul(
                                        PS8[:, bank * 512:(bank + 1) * 512],
                                        up_sb[spar][:, dc * 512 + hc * 128:dc * 512 + (hc + 1) * 128],
                                        xnT[dc][:, t0 + ts_ * 512:t0 + (ts_ + 1) * 512],
                                        start=(dc == 0), stop=(dc == DC - 1)),
                                        s_pe if dc == DC - 1 else None, 1)
                                pv = w.v(s_pe)
                                w.do("scalar", lambda e, hchunk=hchunk, ts_=ts_, bank=bank, e_i=e_i, pv=pv: (
                                    e.wait_ge(s_pe, pv),
                                    e.activation(out=hT[hchunk][:, ts_ * 512:(ts_ + 1) * 512],
                                                 in_=PS8[:, bank * 512:(bank + 1) * 512],
                                                 func=AF.Gelu,
                                                 bias=bup_c[:, e_i * 32 + hchunk:e_i * 32 + hchunk + 1],
                                                 scale=1.0))[-1], s_act, 1)
                                upbank_rd[bank] = w.v(s_act)
                        up_rd[spar] = w.v(s_pe)
                    gelu_done = w.v(s_act)

                    for half in range(2):
                        for hq in range(8):
                            spar = hq % 2
                            first_visit = (tile == 0 and half == 0)
                            role = "gpsimd" if first_visit else "sync"
                            w.wait(role, s_pe, dn_rd[spar])
                            if first_visit:
                                w.wait("gpsimd", s_ws, cap3.get(("dnw", spar), 0))
                                w.do("gpsimd", lambda e, e_i=e_i, hq=hq, spar=spar: e.dma_start(
                                    out=dn_sb[spar][:].rearrange("p (hc d) -> p hc d", hc=4),
                                    in_=Wdn_t.ap()[e_i].rearrange(
                                        "(hq hc p) d -> hq p hc d", hc=4, p=128)[hq]), s_dng[spar], 16)
                                slab_ready = w.v(s_dng[spar])
                                w.wait("sync", s_dng[spar], slab_ready)
                                w.do("sync", lambda e, e_i=e_i, hq=hq, spar=spar: e.dma_start(
                                    out=WDNscr.ap()[e_i, hq], in_=dn_sb[spar][:]), s_ws, 16)
                                cap3[("dnw", spar)] = w.v(s_ws)
                                cap3[("dnwv", e_i, hq)] = w.v(s_ws)
                                slab_sem, slab_val = s_dng[spar], slab_ready
                            else:
                                w.wait("sync", s_ws, cap3.get(("dnwv", e_i, hq), 0))
                                w.do("sync", lambda e, e_i=e_i, hq=hq, spar=spar: e.dma_start(
                                    out=dn_sb[spar][:], in_=WDNscr.ap()[e_i, hq]), s_dnr[spar], 16)
                                slab_sem, slab_val = s_dnr[spar], w.v(s_dnr[spar])
                            for hc in range(4):
                                hchunk = hq * 4 + hc
                                w.wait("tensor", slab_sem, slab_val)
                                if hchunk == 0:
                                    w.wait("tensor", s_act, gelu_done)
                                    for b in range(8):
                                        w.wait("tensor", s_act, ybank_rd[b])
                                for dout in range(DC):
                                    w.do("tensor", lambda e, hc=hc, dout=dout, half=half, spar=spar, hchunk=hchunk: e.matmul(
                                        PS8[:, dout * 512:(dout + 1) * 512],
                                        dn_sb[spar][:, (hc * DC + dout) * 128:(hc * DC + dout + 1) * 128],
                                        hT[hchunk][:, half * 512:(half + 1) * 512],
                                        start=(hchunk == 0), stop=(hchunk == HCN - 1)),
                                        s_pe if (hc == 3 and dout == DC - 1) else None, 1)
                            hq_done = w.v(s_pe)
                            dn_rd[spar] = hq_done
                            if half == 1:
                                for hc in range(4):
                                    hT_rd[hq * 4 + hc] = hq_done
                        pv = w.v(s_pe)
                        for di, dout in enumerate((0, 6, 7, 1, 2, 3, 4, 5)):
                            yp = di % 2
                            w.wait("scalar", s_pe, pv)
                            w.wait("scalar", s_dve, ysb_rd[yp])
                            w.do("scalar", lambda e, dout=dout, yp=yp, e_i=e_i: e.activation(
                                out=ysb[yp][:], in_=PS8[:, dout * 512:(dout + 1) * 512],
                                func=AF.Identity,
                                bias=bdn_c[:, e_i * 8 + dout:e_i * 8 + dout + 1], scale=1.0),
                                s_act, 1)
                            ybank_rd[dout] = w.v(s_act)
                            av = w.v(s_act)
                            w.wait("vector", s_act, max(av, cwbc_ready))
                            if e_i == 0 and half == 0 and di == 0:
                                w.wait("vector", s_acc, acc_ready)
                            w.do("vector", lambda e, yp=yp, epar=epar, half=half: e.tensor_tensor(
                                out=ysb[yp][:], in0=ysb[yp][:],
                                in1=cwbc[epar][:, half * 512:(half + 1) * 512], op=ALU.mult), None)
                            w.do("vector", lambda e, dout=dout, yp=yp, half=half: e.tensor_tensor(
                                out=acc[dout][:, half * 512:(half + 1) * 512],
                                in0=acc[dout][:, half * 512:(half + 1) * 512],
                                in1=ysb[yp][:], op=ALU.add), s_dve, 1)
                            ysb_rd[yp] = w.v(s_dve)
                            if e_i == E - 1:
                                acc_wr_last[dout] = max(acc_wr_last[dout], w.v(s_dve))
                    cwbc_rd[epar] = w.v(s_dve)
                    for b in range(6):
                        upbank_rd[b] = max(upbank_rd[b], ybank_rd[b])

                if _dbg3 and tile == 0:
                    w.wait("sync", s_dve)
                    for c in range(DC):
                        w.do("sync", lambda e, c=c: e.dma_start(
                            out=ACCdbg.ap()[c], in_=acc[c][:]), s_pro, 16)
                    w.wait("sync", s_act)
                    w.do("sync", lambda e: e.dma_start(out=CWdbg.ap(), in_=cwT[0:16, :]), s_pro, 16)
                    for c in range(4):
                        w.do("sync", lambda e, c=c: e.dma_start(
                            out=HTdbg.ap()[c], in_=hT[c][:]), s_pro, 16)
                cap3["acc_dve"] = w.v(s_dve)
                for sl8 in range(8):
                    qt = tile * 8 + sl8
                    for c in range(DC):
                        w.wait("tensor", s_dve, acc_wr_last[c])
                        w.wait("tensor", s_act)
                        w.do("tensor", lambda e, c=c, sl8=sl8: e.transpose(
                            PS8[:, 0:128], acc[c][:, sl8 * 128:(sl8 + 1) * 128], identf[:]),
                            s_pe, 1)
                        pv = w.v(s_pe)
                        w.do("scalar", lambda e, c=c, pv=pv,
                             prd=cap3.get("otok_dve", 0): (
                            e.wait_ge(s_pe, pv),
                            e.wait_ge(s_dve, prd),
                            e.activation(out=r2_tok[:, c * 128:(c + 1) * 128],
                                         in_=PS8[:, 0:128], func=AF.Identity,
                                         bias=0.0, scale=1.0))[-1], s_act, 1)
                    av = w.v(s_act)
                    w.wait("vector", s_act, av)
                    w.do("vector", lambda e: e.reduce_sum(
                        ln3[:, 0:1], r2_tok[:], axis=AX.X), s_dve, 1)
                    dv = w.v(s_dve)
                    op = sl8 % 2
                    w.do("scalar", lambda e, dv=dv, op=op, prd=otok_rd[op]: (
                        e.wait_ge(s_dve, dv),
                        e.wait_ge(s_out, prd),
                        e.activation(out=o_tok[op][:], in_=r2_tok[:], func=AF.Square,
                                     accum_out=ln3[:, 1:2]))[-1], s_act, 1)
                    av = w.v(s_act)
                    w.wait("vector", s_act, av)
                    w.wait("vector", s_dve)
                    w.do("vector", lambda e: e.tensor_scalar(
                        out=ln3[:, 2:3], in0=ln3[:, 0:1], scalar1=1.0 / D, scalar2=None,
                        op0=ALU.mult), s_dve, 1)
                    w.do("vector", lambda e: e.tensor_scalar(
                        out=ln3[:, 3:4], in0=ln3[:, 1:2], scalar1=1.0 / D, scalar2=None,
                        op0=ALU.mult), s_dve, 1)
                    w.wait("vector", s_dve)
                    w.do("vector", lambda e: e.tensor_tensor(
                        out=ln3[:, 4:5], in0=ln3[:, 2:3], in1=ln3[:, 2:3], op=ALU.mult), s_dve, 1)
                    w.wait("vector", s_dve)
                    w.do("vector", lambda e: e.tensor_tensor(
                        out=ln3[:, 5:6], in0=ln3[:, 3:4], in1=ln3[:, 4:5], op=ALU.subtract), s_dve, 1)
                    dv = w.v(s_dve)
                    w.do("scalar", lambda e, dv=dv: (
                        e.wait_ge(s_dve, dv),
                        e.activation(out=ln3[:, 6:7], in_=ln3[:, 5:6], func=AF.Ln,
                                     bias=eps_sb[:], scale=1.0))[-1], None)
                    w.do("scalar", lambda e: e.activation(
                        out=ln3[:, 7:8], in_=ln3[:, 6:7], func=AF.Exp, bias=0.0,
                        scale=-0.5), s_act, 1)
                    av = w.v(s_act)
                    w.wait("vector", s_act, av)
                    w.wait("vector", s_out, otok_rd[op])
                    w.do("vector", lambda e, op=op: e.tensor_scalar(
                        out=o_tok[op][:], in0=r2_tok[:], scalar1=ln3[:, 2:3],
                        scalar2=ln3[:, 7:8], op0=ALU.subtract, op1=ALU.mult), s_dve, 1)
                    dv = w.v(s_dve)
                    cap3["otok_dve"] = dv
                    w.wait("sync", s_dve, dv)
                    w.do("sync", lambda e, qt=qt, op=op: e.dma_start(
                        out=out_ap[qt], in_=o_tok[op][:]), s_out, 16)
                    otok_rd[op] = w.v(s_out)
                acc_pe_rd = w.v(s_pe)
            for _s in (s_pro, s_bias, s_wslab, s_gx[0], s_gx[1], s_kst[0],
                       s_kst[1], s_vst[0], s_vst[1], s_kr[0], s_kr[1], s_xo,
                       s_bo, s_wo, s_xnf, s_xnb, s_xnt, s_acc, s_ws,
                       s_upg[0], s_upg[1], s_upg[2], s_upr[0], s_upr[1],
                       s_upr[2], s_dng[0], s_dng[1], s_dng[2],
                       s_dnr[0], s_dnr[1], s_dnr[2], s_cw):
                w.wait("sync", _s)
            w.wait("sync", s_out, 16 * NSLOT)

        run_phase(phase3)
        p3.close()

    return nc


_NC_CACHE = []


def kernel(**inputs):
    x = np.ascontiguousarray(np.asarray(inputs["L0_attn_x"], dtype=np.float32))
    weights = {
        "Wq": inputs["L0_attn_Wq"], "bq": inputs["L0_attn_bq"],
        "Wk": inputs["L0_attn_Wk"], "bk": inputs["L0_attn_bk"],
        "Wv": inputs["L0_attn_Wv"], "bv": inputs["L0_attn_bv"],
        "Wo": inputs["L0_attn_Wo"], "bo": inputs["L0_attn_bo"],
        "Wg": inputs["L0_ffn_Wg"], "Wup": inputs["L0_ffn_Wup"],
        "bup": inputs["L0_ffn_bup"], "Wdn": inputs["L0_ffn_Wdown"],
        "bdn": inputs["L0_ffn_bdown"],
    }
    weights = {k: np.ascontiguousarray(np.asarray(v, dtype=np.float32))
               for k, v in weights.items()}
    for k in ("bq", "bk", "bv", "bo"):
        weights[k] = weights[k].reshape(1, D)

    in_maps = []
    slot_lists = []
    for c in range(8):
        b = c // 2
        slots = SLOTS_EVEN if c % 2 == 0 else SLOTS_ODD
        slot_lists.append(slots)
        xo = np.stack([x[b, i * 128:(i + 1) * 128, :] for i in slots])
        m = dict(weights)
        m["x_own"] = np.ascontiguousarray(xo)
        m["x_ctx"] = np.ascontiguousarray(x[b])
        m["rowoff"] = np.array([[128.0 * i for i in slots]], dtype=np.float32)
        m["rowoffn"] = np.array([[-128.0 * i for i in slots]], dtype=np.float32)
        in_maps.append(m)

    if not _NC_CACHE:
        _NC_CACHE.append(build_nc())
    nc = _NC_CACHE[0]
    res = run_bass_kernel_spmd(nc, in_maps, core_ids=list(range(8)))

    y = np.empty((4, 4096, D), dtype=np.float32)
    for c in range(8):
        b = c // 2
        o = np.asarray(res.results[c]["out"], dtype=np.float32)
        for si, i in enumerate(slot_lists[c]):
            y[b, i * 128:(i + 1) * 128, :] = o[si]
    return y


if __name__ == "__main__":
    rng = np.random.default_rng(0)
    ins = {
        "L0_attn_x": rng.standard_normal((4, 4096, 1024), dtype=np.float32),
    }
    for n, sh in (("Wq", (D, D)), ("Wk", (D, D)), ("Wv", (D, D)), ("Wo", (D, D)),
                  ("bq", (D,)), ("bk", (D,)), ("bv", (D,)), ("bo", (D,))):
        ins[f"L0_attn_{n}"] = (rng.standard_normal(sh, dtype=np.float32) / 32.0)
    ins["L0_ffn_Wg"] = rng.standard_normal((D, E), dtype=np.float32) * 0.02
    ins["L0_ffn_Wup"] = rng.standard_normal((E, D, HF), dtype=np.float32) * 0.02
    ins["L0_ffn_bup"] = rng.standard_normal((E, HF), dtype=np.float32) * 0.02
    ins["L0_ffn_Wdown"] = rng.standard_normal((E, HF, D), dtype=np.float32) * 0.02
    ins["L0_ffn_bdown"] = rng.standard_normal((E, D), dtype=np.float32) * 0.02
    out = kernel(**ins)
    print("ok", out.shape, out.dtype, np.abs(out).mean())


# version 18 (final)
# speedup vs baseline: 1.1961x; 1.1961x over previous
import os
from collections import defaultdict
from contextlib import ExitStack

import numpy as np

import concourse.bass as bass
import concourse.mybir as mybir
from concourse.bass_utils import run_bass_kernel_spmd

F32 = mybir.dt.float32
BF16 = mybir.dt.bfloat16
AF = mybir.ActivationFunctionType
ALU = mybir.AluOpType
AX = mybir.AxisListType

D = 1024
DC = 8
NCTX = 4096
NSLOT = 16
NSS = 4
HPN = 8
E = 12
HF = 4096
HCN = 32
EPS = 1e-5
NEG = -1e30

SLOTS_EVEN = [0, 3, 4, 7, 8, 11, 12, 15, 16, 19, 20, 23, 24, 27, 28, 31]
SLOTS_ODD = [1, 2, 5, 6, 9, 10, 13, 14, 17, 18, 21, 22, 25, 26, 29, 30]


def klen(s):
    return 256 * (s + 1)


def cdiv(a, b):
    return -(-a // b)


class Walk:

    def __init__(self, role, eng, cnt):
        self.role = role
        self.eng = eng
        self.cnt = cnt

    def do(self, role, fn, inc=None, n=1):
        if self.role == role:
            inst = fn(self.eng)
            if inc is not None and inst is not None:
                inst.then_inc(inc, n)
        if inc is not None:
            self.cnt[inc.name] += n
            return self.cnt[inc.name]
        return None

    def wait(self, role, sem, val=None):
        if val is None:
            val = self.cnt[sem.name]
        if val <= 0:
            return
        if self.role == role:
            self.eng.wait_ge(sem, val)

    def v(self, sem):
        return self.cnt[sem.name]


def build_nc():
    nc = bass.Bass("TRN2", target_bir_lowering=False,
                   detect_race_conditions=(os.environ.get("MK_RACE", "0") == "1"))

    x_own_t = nc.dram_tensor("x_own", [NSLOT, 128, D], F32, kind="ExternalInput")
    x_ctx_t = nc.dram_tensor("x_ctx", [NCTX, D], F32, kind="ExternalInput")
    rowoff_t = nc.dram_tensor("rowoff", [1, NSLOT], F32, kind="ExternalInput")
    rowoffn_t = nc.dram_tensor("rowoffn", [1, NSLOT], F32, kind="ExternalInput")
    Wq_t = nc.dram_tensor("Wq", [D, D], F32, kind="ExternalInput")
    Wk_t = nc.dram_tensor("Wk", [D, D], F32, kind="ExternalInput")
    Wv_t = nc.dram_tensor("Wv", [D, D], F32, kind="ExternalInput")
    Wo_t = nc.dram_tensor("Wo", [D, D], F32, kind="ExternalInput")
    bq_t = nc.dram_tensor("bq", [1, D], F32, kind="ExternalInput")
    bk_t = nc.dram_tensor("bk", [1, D], F32, kind="ExternalInput")
    bv_t = nc.dram_tensor("bv", [1, D], F32, kind="ExternalInput")
    bo_t = nc.dram_tensor("bo", [1, D], F32, kind="ExternalInput")
    Wg_t = nc.dram_tensor("Wg", [D, E], F32, kind="ExternalInput")
    Wup_t = nc.dram_tensor("Wup", [E, D, HF], F32, kind="ExternalInput")
    bup_t = nc.dram_tensor("bup", [E, HF], F32, kind="ExternalInput")
    Wdn_t = nc.dram_tensor("Wdn", [E, HF, D], F32, kind="ExternalInput")
    bdn_t = nc.dram_tensor("bdn", [E, D], F32, kind="ExternalInput")
    out_t = nc.dram_tensor("out", [NSLOT, 128, D], F32, kind="ExternalOutput")

    _scratch_kind = "ExternalOutput" if os.environ.get("MK_DUMP") else "Internal"
    Kscr = nc.dram_tensor("Kscr", [HPN, 128, NCTX], BF16, kind=_scratch_kind)
    Vscr = nc.dram_tensor("Vscr", [HPN, NCTX, 128], BF16, kind=_scratch_kind)
    Qscr = nc.dram_tensor("Qscr", [HPN, 128, NSLOT * 128], BF16, kind=_scratch_kind)
    XNscr = nc.dram_tensor("XNscr", [NSLOT, DC, 128, 128], F32, kind=_scratch_kind)
    XNBFscr = nc.dram_tensor("XNBFscr", [NSLOT, DC, 128, 128], BF16, kind=_scratch_kind)
    _dbg2 = os.environ.get("MK_DBG2")
    _dbg3 = os.environ.get("MK_DBG3")
    if _dbg3:
        ACCdbg = nc.dram_tensor("ACCdbg", [DC, 128, 1024], F32, kind="ExternalOutput")
        CWdbg = nc.dram_tensor("CWdbg", [16, 2048], BF16, kind="ExternalOutput")
        HTdbg = nc.dram_tensor("HTdbg", [4, 128, 1024], BF16, kind="ExternalOutput")
    if _dbg2:
        AOdbg = nc.dram_tensor("AOdbg", [DC, 128, 512], BF16, kind="ExternalOutput")
        POdbg = nc.dram_tensor("POdbg", [4, 128, D], F32, kind="ExternalOutput")
        REdbg = nc.dram_tensor("REdbg", [4, 128, D], F32, kind="ExternalOutput")
        Pdbg = nc.dram_tensor("Pdbg", [2, 128, NCTX], BF16, kind="ExternalOutput")
    WUPscr = nc.dram_tensor("WUPscr", [E, 8, 128, 4096], BF16, kind="Internal")
    WDNscr = nc.dram_tensor("WDNscr", [E, 8, 128, 4096], BF16, kind="Internal")

    x_own = x_own_t.ap()
    x_ctx = x_ctx_t.ap()
    out_ap = out_t.ap()

    ctx = ExitStack()

    def sb(name, shape, dt=F32):
        return ctx.enter_context(nc.sbuf_tensor(name, shape, dt))

    def sem(name):
        return ctx.enter_context(nc.semaphore(name))

    with ctx:
        s_pe = sem("s_pe")
        s_act = sem("s_act")
        s_dve = sem("s_dve")
        s_pool = sem("s_pool")
        s_pro = sem("s_pro")
        s_bias = sem("s_bias")
        s_wslab = sem("s_wslab")
        s_gx = [sem(f"s_gx{i}") for i in range(2)]
        s_kst = [sem(f"s_kst{i}") for i in range(2)]
        s_vst = [sem(f"s_vst{i}") for i in range(2)]
        s_kr = [sem(f"s_kr{i}") for i in range(2)]
        s_xo = sem("s_xo")
        s_bo = sem("s_bo")
        s_wo = sem("s_wo")
        s_xnf = sem("s_xnf")
        s_xnb = sem("s_xnb")
        s_xnt = sem("s_xnt")
        s_acc = sem("s_acc")
        s_ws = sem("s_ws")
        s_upg = [sem(f"s_upg{i}") for i in range(3)]
        s_upr = [sem(f"s_upr{i}") for i in range(3)]
        s_dng = [sem(f"s_dng{i}") for i in range(3)]
        s_dnr = [sem(f"s_dnr{i}") for i in range(3)]
        s_cw = sem("s_cw")
        s_out = sem("s_out")

        cnt0 = defaultdict(int)

        ident_bf = sb("ident_bf", [128, 128], BF16)
        identf = sb("identf", [128, 128], F32)
        ones_bf = sb("ones_bf", [1, 128], BF16)
        rowoff_bc = sb("rowoff_bc", [128, NSLOT], F32)
        rowoffn_bc = sb("rowoffn_bc", [128, NSLOT], F32)
        eps_sb = sb("eps_sb", [128, 1], F32)
        bup_c = sb("bup_c", [128, E * 32], F32)
        bdn_c = sb("bdn_c", [128, E * 8], F32)
        wg_sb = sb("wg_sb", [128, DC * E], F32)

        phase_state = [dict(cnt0)]

        def run_phase(phase_fn):
            results = {}
            start = dict(phase_state[0])
            with nc.Block() as blk:
                for role, decor in (("sync", blk.sync), ("tensor", blk.tensor),
                                    ("scalar", blk.scalar), ("vector", blk.vector),
                                    ("gpsimd", blk.gpsimd)):
                    def make(role=role):
                        def f(eng):
                            w = Walk(role, eng, defaultdict(int, dict(start)))
                            phase_fn(w)
                            results[role] = dict(w.cnt)
                        return f
                    decor(make())
            vals = list(results.values())
            for v in vals[1:]:
                assert v == vals[0], f"walk divergence: {v} vs {vals[0]}"
            phase_state[0] = vals[0]

        p1 = ExitStack()
        bv_bc_f = p1.enter_context(nc.sbuf_tensor("bv_bc_f", [128, D], F32))
        bv_bc = p1.enter_context(nc.sbuf_tensor("bv_bc", [128, D], BF16))
        biasr = p1.enter_context(nc.sbuf_tensor("biasr", [128, 128], F32))
        bq_c = p1.enter_context(nc.sbuf_tensor("bq_c", [128, 8], F32))
        bk_c = p1.enter_context(nc.sbuf_tensor("bk_c", [128, 8], F32))
        wq_sb = [p1.enter_context(nc.sbuf_tensor(f"wq{c}", [128, D], BF16)) for c in range(DC)]
        wk_sb = [p1.enter_context(nc.sbuf_tensor(f"wk{c}", [128, D], BF16)) for c in range(DC)]
        wv_sb = [p1.enter_context(nc.sbuf_tensor(f"wv{c}", [128, D], BF16)) for c in range(DC)]
        xt = [[p1.enter_context(nc.sbuf_tensor(f"xt{p}_{j}", [128, D], BF16))
               for j in range(4)] for p in range(2)]
        Xg = [[p1.enter_context(nc.sbuf_tensor(f"Xg{p}_{c}", [128, 512], BF16))
               for c in range(DC)] for p in range(2)]
        Ksta = [p1.enter_context(nc.sbuf_tensor(f"Ksta{p}", [128, 512], BF16)) for p in range(2)]
        Vsta = [p1.enter_context(nc.sbuf_tensor(f"Vsta{p}", [128, 512], BF16)) for p in range(2)]
        Vstb = [p1.enter_context(nc.sbuf_tensor(f"Vstb{p}", [128, 512], BF16)) for p in range(2)]
        ps_f = p1.enter_context(nc.psum_tensor("ps_f", [128, 6 * 512], F32))
        ps_t = [p1.enter_context(nc.psum_tensor(f"ps_t{i}", [128, 512], BF16)) for i in range(2)]

        def phase1(w):
            cap = {}
            w.do("sync", lambda e: e.dma_start(
                out=rowoff_bc[:], in_=bass.AP(rowoff_t, 0, [[0, 128], [1, NSLOT]])), s_pro, 16)
            w.do("sync", lambda e: e.dma_start(
                out=rowoffn_bc[:], in_=bass.AP(rowoffn_t, 0, [[0, 128], [1, NSLOT]])), s_pro, 16)
            w.do("sync", lambda e: e.dma_start(
                out=bv_bc_f[:], in_=bass.AP(bv_t, 0, [[0, 128], [1, D]])), s_pro, 16)
            cap["pro_sy"] = w.v(s_pro)

            w.do("gpsimd", lambda e: e.memset(ident_bf[:], 0.0), s_pool, 1)
            w.do("gpsimd", lambda e: e.memset(identf[:], 0.0), s_pool, 1)
            w.wait("gpsimd", s_pool)
            w.do("gpsimd", lambda e: e.affine_select(
                out=ident_bf[:], in_=ident_bf[:], compare_op=ALU.not_equal,
                fill=1.0, base=0, pattern=[[-1, 128]], channel_multiplier=1), s_pool, 1)
            w.do("gpsimd", lambda e: e.affine_select(
                out=identf[:], in_=identf[:], compare_op=ALU.not_equal,
                fill=1.0, base=0, pattern=[[-1, 128]], channel_multiplier=1), s_pool, 1)
            w.do("sync", lambda e: e.dma_start(
                out=wg_sb[:].rearrange("p (dc e) -> p dc e", dc=DC),
                in_=Wg_t.ap().rearrange("(dc p) e -> p dc e", p=128)), s_pro, 16)
            for c in range(DC):
                for wsb, wt in ((wq_sb, Wq_t), (wk_sb, Wk_t), (wv_sb, Wv_t)):
                    w.do("gpsimd", lambda e, wsb=wsb, wt=wt, c=c: e.dma_start(
                        out=wsb[c][:], in_=wt.ap()[c * 128:(c + 1) * 128, :]), s_wslab, 16)
            cap["w_gw"] = w.v(s_wslab)

            w.do("vector", lambda e: e.memset(ones_bf[:], 1.0))
            w.do("vector", lambda e: e.memset(eps_sb[:], EPS))
            w.wait("vector", s_pro, cap["pro_sy"])
            w.do("vector", lambda e: e.tensor_copy(out=bv_bc[:], in_=bv_bc_f[:]))

            bias_jobs = [(bq_t, bq_c, 0.125), (bk_t, bk_c, 1.0)]
            for bt, bcol, scale in bias_jobs:
                w.wait("sync", s_pe)
                w.do("sync", lambda e, bt=bt: e.dma_start(
                    out=biasr[0:8, :], in_=bt.ap().rearrange("o (c p) -> (o c) p", p=128)), s_bias, 16)
                w.wait("tensor", s_bias)
                w.wait("tensor", s_pool, 4)
                w.wait("tensor", s_act)
                w.do("tensor", lambda e: e.transpose(ps_f[:, 0:8], biasr[0:8, :], identf[0:8, 0:8]), s_pe, 1)
                w.do("scalar", lambda e, bcol=bcol, scale=scale: (
                    e.wait_ge(s_pe, w.v(s_pe)),
                    e.activation(out=bcol[:], in_=ps_f[:, 0:8], func=AF.Identity,
                                 bias=0.0, scale=scale))[-1], s_act, 1)
            for j in range(3):
                w.wait("sync", s_pe)
                w.do("sync", lambda e, j=j: e.dma_start(
                    out=biasr[:, :],
                    in_=bup_t.ap().rearrange("e (c p) -> (e c) p", p=128)[j * 128:(j + 1) * 128, :]),
                    s_bias, 16)
                w.wait("tensor", s_bias)
                w.wait("tensor", s_act)
                w.do("tensor", lambda e: e.transpose(ps_f[:, 0:128], biasr[:, :], identf[:]), s_pe, 1)
                w.do("scalar", lambda e, j=j: (
                    e.wait_ge(s_pe, w.v(s_pe)),
                    e.activation(out=bup_c[:, j * 128:(j + 1) * 128], in_=ps_f[:, 0:128],
                                 func=AF.Identity, bias=0.0, scale=1.0))[-1], s_act, 1)
            w.wait("sync", s_pe)
            w.do("sync", lambda e: e.dma_start(
                out=biasr[0:96, :], in_=bdn_t.ap().rearrange("e (c p) -> (e c) p", p=128)), s_bias, 16)
            w.wait("tensor", s_bias)
            w.wait("tensor", s_act)
            w.do("tensor", lambda e: e.transpose(ps_f[:, 0:96], biasr[0:96, :], identf[0:96, 0:96]), s_pe, 1)
            w.do("scalar", lambda e: (
                e.wait_ge(s_pe, w.v(s_pe)),
                e.activation(out=bdn_c[:], in_=ps_f[:, 0:96], func=AF.Identity,
                             bias=0.0, scale=1.0))[-1], s_act, 1)

            tp_evac = []
            xt_read = {}

            def do_group(g, src_is_own, kjob, vjob, qjob):
                par = g % 2
                w.wait("gpsimd", s_pe, xt_read.get(par, 0))
                for j in range(4):
                    if src_is_own:
                        src = x_own[(g - 8) * 4 + j]
                    else:
                        src = x_ctx[(g * 4 + j) * 128:(g * 4 + j + 1) * 128, :]
                    w.do("gpsimd", lambda e, j=j, src=src: e.dma_start(
                        out=xt[par][j][:], in_=src), s_gx[par], 16)
                gx_done = w.v(s_gx[par])
                w.wait("tensor", s_gx[par], gx_done)
                ti = 0
                for j in range(4):
                    for c in range(DC):
                        bnk = ti % 2
                        col = (ti // 2) % 4
                        if len(tp_evac) >= 2:
                            w.wait("tensor", s_act, tp_evac[len(tp_evac) - 2])
                        w.do("tensor", lambda e, j=j, c=c, bnk=bnk, col=col: e.transpose(
                            ps_t[bnk][:, col * 128:(col + 1) * 128],
                            xt[par][j][:, c * 128:(c + 1) * 128], ident_bf[:]), s_pe, 1)
                        pv = w.v(s_pe)
                        w.do("scalar", lambda e, j=j, c=c, bnk=bnk, col=col, pv=pv: (
                            e.wait_ge(s_pe, pv),
                            e.activation(out=Xg[par][c][:, j * 128:(j + 1) * 128],
                                         in_=ps_t[bnk][:, col * 128:(col + 1) * 128],
                                         func=AF.Identity, bias=0.0, scale=1.0))[-1], s_act, 1)
                        tp_evac.append(w.v(s_act))
                        ti += 1
                xg_ready = w.v(s_act)
                xt_read[par] = w.v(s_pe)

                if kjob:
                    wsb, scr, bcol, wsem, scale = kjob
                    for dout in range(DC):
                        bank = dout % 2
                        w.wait("tensor", s_act, xg_ready)
                        w.wait("tensor", s_act)
                        for dc in range(DC):
                            w.do("tensor", lambda e, dc=dc, dout=dout, bank=bank, wsb=wsb: e.matmul(
                                ps_f[:, bank * 512:(bank + 1) * 512],
                                wsb[dc][:, dout * 128:(dout + 1) * 128],
                                Xg[par][dc][:], start=(dc == 0), stop=(dc == DC - 1)),
                                s_pe if dc == DC - 1 else None, 1)
                        pv = w.v(s_pe)
                        spar = dout % 2
                        w.wait("scalar", s_kst[spar], cap.get(("ksta", spar), 0))
                        w.do("scalar", lambda e, dout=dout, bank=bank, spar=spar, scale=scale, bcol=bcol, pv=pv: (
                            e.wait_ge(s_pe, pv),
                            e.activation(out=Ksta[spar][:], in_=ps_f[:, bank * 512:(bank + 1) * 512],
                                         func=AF.Identity, bias=bcol[:, dout:dout + 1],
                                         scale=scale))[-1], s_act, 1)
                        av = w.v(s_act)
                        w.wait("sync", s_act, av)
                        w.do("sync", lambda e, dout=dout, spar=spar, scr=scr: e.dma_start(
                            out=scr.ap()[dout][:, g * 512:g * 512 + 512],
                            in_=Ksta[spar][:]), s_kst[spar], 16)
                        cap[("ksta", spar)] = w.v(s_kst[spar])
                if vjob:
                    for ksub in range(4):
                        for half in range(2):
                            for dc in range(DC):
                                for d4 in range(4):
                                    dout = half * 4 + d4
                                    bank = 2 + d4
                                    if dc == 0:
                                        w.wait("tensor", s_act, xg_ready)
                                        w.wait("tensor", s_dve)
                                    w.do("tensor", lambda e, dc=dc, ksub=ksub, dout=dout, bank=bank: e.matmul(
                                        ps_f[:, bank * 512:bank * 512 + 128],
                                        Xg[par][dc][:, ksub * 128:(ksub + 1) * 128],
                                        wv_sb[dc][:, dout * 128:(dout + 1) * 128],
                                        start=(dc == 0), stop=(dc == DC - 1)),
                                        s_pe if dc == DC - 1 and d4 == 3 else None, 1)
                            pv = w.v(s_pe)
                            for d4 in range(4):
                                dout = half * 4 + d4
                                bank = 2 + d4
                                spar = d4 % 2
                                w.wait("scalar", s_dve, cap.get(("vsta", spar), 0))
                                w.do("scalar", lambda e, bank=bank, spar=spar, d4=d4, pv=pv: (
                                    e.wait_ge(s_pe, pv),
                                    e.activation(out=Vsta[spar][:, 0:128],
                                                 in_=ps_f[:, bank * 512:bank * 512 + 128],
                                                 func=AF.Identity, bias=0.0, scale=1.0))[-1], s_act, 1)
                                av = w.v(s_act)
                                w.wait("vector", s_vst[spar], cap.get(("vstb", spar), 0))
                                w.do("vector", lambda e, spar=spar, dout=dout, av=av: (
                                    e.wait_ge(s_act, av),
                                    e.tensor_tensor(out=Vstb[spar][:, 0:128], in0=Vsta[spar][:, 0:128],
                                                    in1=bv_bc[:, dout * 128:(dout + 1) * 128],
                                                    op=ALU.add))[-1], s_dve, 1)
                                cap[("vsta", spar)] = w.v(s_dve)
                                dv = w.v(s_dve)
                                w.wait("sync", s_dve, dv)
                                w.do("sync", lambda e, dout=dout, spar=spar, ksub=ksub: e.dma_start(
                                    out=Vscr.ap()[dout][g * 512 + ksub * 128:g * 512 + (ksub + 1) * 128, :],
                                    in_=Vstb[spar][:, 0:128]), s_vst[spar], 16)
                                cap[("vstb", spar)] = w.v(s_vst[spar])
                if qjob:
                    wsb, scr, bcol, wsem, scale = qjob
                    for dout in range(DC):
                        bank = dout % 2
                        w.wait("tensor", s_act, xg_ready)
                        w.wait("tensor", s_act)
                        for dc in range(DC):
                            w.do("tensor", lambda e, dc=dc, dout=dout, bank=bank, wsb=wsb: e.matmul(
                                ps_f[:, bank * 512:(bank + 1) * 512],
                                wsb[dc][:, dout * 128:(dout + 1) * 128],
                                Xg[par][dc][:], start=(dc == 0), stop=(dc == DC - 1)),
                                s_pe if dc == DC - 1 else None, 1)
                        pv = w.v(s_pe)
                        spar = dout % 2
                        w.wait("scalar", s_kst[spar], cap.get(("ksta", spar), 0))
                        w.do("scalar", lambda e, dout=dout, bank=bank, spar=spar, scale=scale, bcol=bcol, pv=pv: (
                            e.wait_ge(s_pe, pv),
                            e.activation(out=Ksta[spar][:], in_=ps_f[:, bank * 512:(bank + 1) * 512],
                                         func=AF.Identity, bias=bcol[:, dout:dout + 1],
                                         scale=scale))[-1], s_act, 1)
                        av = w.v(s_act)
                        w.wait("sync", s_act, av)
                        w.do("sync", lambda e, dout=dout, spar=spar, scr=scr: e.dma_start(
                            out=scr.ap()[dout][:, (g - 8) * 512:(g - 8) * 512 + 512],
                            in_=Ksta[spar][:]), s_kst[spar], 16)
                        cap[("ksta", spar)] = w.v(s_kst[spar])

            w.wait("tensor", s_wslab, cap["w_gw"])
            _g1 = int(os.environ.get("MK_P1GROUPS", "8"))
            _q1 = int(os.environ.get("MK_P1Q", "1"))
            for g in range(_g1):
                do_group(g, False,
                         kjob=(wk_sb, Kscr, bk_c, None, 1.0),
                         vjob=int(os.environ.get("MK_P1V", "1")) > 0, qjob=None)
            if _q1:
                for g in range(8, 12):
                    do_group(g, True, kjob=None, vjob=False,
                             qjob=(wq_sb, Qscr, bq_c, None, 0.125))
            for _s in (s_kst[0], s_kst[1], s_vst[0], s_vst[1]):
                w.wait("sync", _s)

        _ph = os.environ.get("MK_PHASES", "123")
        run_phase(phase1)
        p1.close()
        if "2" not in _ph:
            return nc

        p2 = ExitStack()
        bo_bc = p2.enter_context(nc.sbuf_tensor("bo_bc", [128, D], F32))
        wo_sb = [p2.enter_context(nc.sbuf_tensor(f"wo{c}", [128, D], BF16)) for c in range(DC)]
        q_sb = [p2.enter_context(nc.sbuf_tensor(f"q{p}", [128, NSLOT * 128], BF16)) for p in range(2)]
        k_sb = [p2.enter_context(nc.sbuf_tensor(f"k{p}", [128, NCTX], BF16)) for p in range(2)]
        v_sb = [p2.enter_context(nc.sbuf_tensor(f"v{p}", [128, NCTX], BF16)) for p in range(2)]
        band = [p2.enter_context(nc.sbuf_tensor(f"band{i}", [128, NCTX], BF16)) for i in range(4)]
        P_sb = [p2.enter_context(nc.sbuf_tensor(f"P{p}", [128, NCTX], BF16)) for p in range(2)]
        tmpA = [p2.enter_context(nc.sbuf_tensor(f"tmpA{i}", [128, 1024], BF16)) for i in range(2)]
        diag_sb = [p2.enter_context(nc.sbuf_tensor(f"diag{i}", [128, 512], BF16)) for i in range(2)]
        pt_sb = [p2.enter_context(nc.sbuf_tensor(f"pt{p}", [128, 512], BF16)) for p in range(2)]
        ao_sb = [[p2.enter_context(nc.sbuf_tensor(f"ao{p}_{c}", [128, 512], BF16))
                  for c in range(DC)] for p in range(2)]
        mgen = [p2.enter_context(nc.sbuf_tensor(f"mgen{i}", [128, 512], F32)) for i in range(3)]
        sp_sb = [p2.enter_context(nc.sbuf_tensor(f"sp{p}", [128, 4], F32)) for p in range(2)]
        nm_sb = [p2.enter_context(nc.sbuf_tensor(f"nm{p}", [128, 6], F32)) for p in range(2)]
        rr_sb = [p2.enter_context(nc.sbuf_tensor(f"rr{p}", [128, 6], F32)) for p in range(2)]
        xo_f = [p2.enter_context(nc.sbuf_tensor("xo0", [128, D], F32))]
        po_sb = p2.enter_context(nc.sbuf_tensor("po_sb", [128, D], F32))
        pt_tok = p2.enter_context(nc.sbuf_tensor("pt_tok", [128, D], F32))
        res_tok = p2.enter_context(nc.sbuf_tensor("res_tok", [128, D], F32))
        xnbf_stage = p2.enter_context(nc.sbuf_tensor("xnbf_stage", [128, D], BF16))
        xn_tok = p2.enter_context(nc.sbuf_tensor("xn_tok", [128, D], F32))
        xn_bf = p2.enter_context(nc.sbuf_tensor("xn_bf", [128, D], BF16))
        xn_fm = p2.enter_context(nc.sbuf_tensor("xn_fm", [128, D], F32))
        ln_s = p2.enter_context(nc.sbuf_tensor("ln_s", [128, 8], F32))
        ps_S = [p2.enter_context(nc.psum_tensor(f"ps_S{i}", [128, 2 * 512], F32)) for i in range(2)]
        ps_PT = [p2.enter_context(nc.psum_tensor(f"ps_PT{i}", [128, 512], F32)) for i in range(2)]
        ps_A = p2.enter_context(nc.psum_tensor("ps_A", [128, 1024], F32))

        def phase2(w):
            _nss = int(os.environ.get("MK_P2SS", str(NSS)))
            cap2 = {}
            bank_rd = [0, 0]
            aA_rd2 = [0, 0]
            pending = []
            diag_rd = [0, 0]

            def pop_steps(n):
                for _ in range(min(n, len(pending))):
                    pending.pop(0)[1]()

            def flush_steps():
                while pending:
                    pending.pop(0)[1]()

            def flush_older(min_keep):
                while pending and pending[0][0] < min_keep:
                    pending.pop(0)[1]()
            pt_evac = [0, 0]
            pt_evacd = [0, 0]
            pt_read = [0, 0]
            kvq_rd = [0, 0]
            p_read = [0, 0]
            ao_rd = [0, 0]
            nm_rd = [0, 0]
            rr_rd = [0, 0]
            xo_rd = [0, 0]
            aA_rd = 0
            unit_idx = 0

            w.do("sync", lambda e: e.dma_start(
                out=bo_bc[:], in_=bass.AP(bo_t, 0, [[0, 128], [1, D]])), s_bo, 16)
            cap2["bo"] = w.v(s_bo)
            for c in range(DC):
                w.do("gpsimd", lambda e, c=c: e.dma_start(
                    out=wo_sb[c][:], in_=Wo_t.ap()[c * 128:(c + 1) * 128, :]), s_wo, 16)
            cap2["wo"] = w.v(s_wo)

            for ss in range(_nss):
                kmax = 1024 * (ss + 1)
                for sl in range(4):
                    s = 4 * ss + sl
                    kl = klen(s)
                    for kc in range(cdiv(kl, 512)):
                        csz = min(512, kl - kc * 512)
                        w.wait("gpsimd", s_dve, cap2.get(("mg", 0), 0))
                        w.do("gpsimd", lambda e, kc=kc, csz=csz: e.iota(
                            mgen[0][:, 0:csz], [[-1, csz]], base=-(512 * kc),
                            channel_multiplier=1, allow_small_or_imprecise_dtypes=True), s_pool, 1)
                        pl = w.v(s_pool)
                        w.wait("vector", s_pool, pl)
                        w.do("vector", lambda e, s=s, csz=csz: e.tensor_scalar(
                            out=mgen[1][:, 0:csz], in0=mgen[0][:, 0:csz],
                            scalar1=rowoff_bc[:, s:s + 1], scalar2=1.0,
                            op0=ALU.add, op1=ALU.max), None)
                        w.do("vector", lambda e, s=s, csz=csz: e.tensor_scalar(
                            out=mgen[0][:, 0:csz], in0=mgen[0][:, 0:csz],
                            scalar1=rowoffn_bc[:, s:s + 1], scalar2=None,
                            op0=ALU.is_ge), s_dve, 1)
                        cap2[("mg", 0)] = w.v(s_dve)
                        dv = w.v(s_dve)
                        w.wait("scalar", s_dve, dv)
                        w.do("scalar", lambda e, csz=csz: e.activation(
                            out=mgen[2][:, 0:csz], in_=mgen[1][:, 0:csz], func=AF.Ln), None)
                        w.do("scalar", lambda e, csz=csz: e.activation(
                            out=mgen[1][:, 0:csz], in_=mgen[2][:, 0:csz], func=AF.Exp,
                            bias=0.0, scale=-0.25), s_act, 1)
                        av = w.v(s_act)
                        w.wait("vector", s_act, av)
                        w.do("vector", lambda e, sl=sl, kc=kc, csz=csz: e.tensor_tensor(
                            out=band[sl][:, kc * 512:kc * 512 + csz], in0=mgen[1][:, 0:csz],
                            in1=mgen[0][:, 0:csz], op=ALU.mult), s_dve, 1)
                        cap2[("mg", 0)] = w.v(s_dve)

                for hp in range(HPN):
                    par = hp % 2
                    w.wait("sync", s_pe, kvq_rd[par])
                    w.do("sync", lambda e, hp=hp, par=par, kmax=kmax: e.dma_start(
                        out=k_sb[par][:, 0:kmax], in_=Kscr.ap()[hp][:, 0:kmax]), s_kr[par], 16)
                    w.do("sync", lambda e, hp=hp, par=par, kmax=kmax: e.dma_start(
                        out=v_sb[par][:, 0:kmax].rearrange("p (kc v) -> p kc v", v=128),
                        in_=Vscr.ap()[hp][0:kmax, :].rearrange("(kc p) v -> p kc v", p=128)),
                        s_kr[par], 16)
                    w.do("sync", lambda e, hp=hp, par=par: e.dma_start(
                        out=q_sb[par][:], in_=Qscr.ap()[hp][:]), s_kr[par], 16)
                    kvq_ready = w.v(s_kr[par])

                    for h in range(2):
                        for sl in range(4):
                            s = 4 * ss + sl
                            kl = klen(s)
                            C = cdiv(kl, 512)
                            Cp = min(C, 5)
                            up2 = unit_idx % 2
                            hp0 = h * 64

                            R = cdiv(kl, 1024)
                            w.wait("vector", s_pe, p_read[up2])
                            w.wait("vector", s_act, nm_rd[up2])
                            for r in range(R):
                                klr = min(1024, kl - r * 1024)
                                for kc in range(cdiv(klr, 512)):
                                    csz = min(512, klr - kc * 512)
                                    w.wait("tensor", s_kr[par], kvq_ready)
                                    w.wait("tensor", s_act, bank_rd[up2])
                                    w.do("tensor", lambda e, r=r, kc=kc, csz=csz: e.matmul(
                                        ps_S[up2][:, kc * 512:kc * 512 + csz],
                                        q_sb[par][hp0:hp0 + 64, s * 128:(s + 1) * 128],
                                        k_sb[par][hp0:hp0 + 64,
                                                  r * 1024 + kc * 512:r * 1024 + kc * 512 + csz],
                                        start=True, stop=True),
                                        s_pe if kc == cdiv(klr, 512) - 1 else None, 1)
                                pv = w.v(s_pe)
                                w.do("vector", lambda e, klr=klr, r=r, pv=pv: (
                                    e.wait_ge(s_pe, pv),
                                    e.reduce_max(nm_sb[up2][:, r:r + 1], ps_S[up2][:, 0:klr],
                                                 axis=AX.X, negate=True))[-1], s_dve, 1)
                                nmv = w.v(s_dve)
                                tp = r % 2
                                w.wait("scalar", s_dve, max(nmv, cap2.get(("tmpA", up2, tp), 0)))
                                w.do("scalar", lambda e, klr=klr, r=r, tp=tp: e.activation(
                                    out=tmpA[tp][:, 0:klr], in_=ps_S[up2][:, 0:klr], func=AF.Exp,
                                    bias=nm_sb[up2][:, r:r + 1], scale=1.0), s_act, 1)
                                bank_rd[up2] = w.v(s_act)
                                av = w.v(s_act)
                                w.do("vector", lambda e, klr=klr, r=r, sl=sl, tp=tp, av=av: (
                                    e.wait_ge(s_act, av),
                                    e.scalar_tensor_tensor(
                                        out=P_sb[up2][:, r * 1024:r * 1024 + klr],
                                        in0=tmpA[tp][:, 0:klr], scalar=1.0,
                                        in1=band[sl][:, r * 1024:r * 1024 + klr],
                                        op0=ALU.mult, op1=ALU.mult,
                                        accum_out=sp_sb[up2][:, r:r + 1]))[-1], s_dve, 1)
                                cap2[("tmpA", up2, tp)] = w.v(s_dve)
                                pop_steps(3)
                            nm_rd[up2] = w.v(s_act)
                            w.wait("vector", s_dve)
                            if R > 1:
                                w.do("vector", lambda e, R=R: e.tensor_reduce(
                                    nm_sb[up2][:, 4:5], nm_sb[up2][:, 0:R], axis=AX.X,
                                    op=ALU.min), s_dve, 1)
                                w.wait("vector", s_dve)
                                w.do("vector", lambda e, R=R: e.tensor_scalar(
                                    out=rr_sb[up2][:, 0:R], in0=nm_sb[up2][:, 0:R],
                                    scalar1=nm_sb[up2][:, 4:5], scalar2=-1.0,
                                    op0=ALU.subtract, op1=ALU.mult), s_dve, 1)
                                dv = w.v(s_dve)
                                w.do("scalar", lambda e, R=R, dv=dv: (
                                    e.wait_ge(s_dve, dv),
                                    e.activation(out=rr_sb[up2][:, 0:R], in_=rr_sb[up2][:, 0:R],
                                                 func=AF.Exp))[-1], s_act, 1)
                                av = w.v(s_act)
                                w.wait("vector", s_act, av)
                                w.do("vector", lambda e, R=R: e.scalar_tensor_tensor(
                                    out=sp_sb[up2][:, 0:R], in0=sp_sb[up2][:, 0:R],
                                    scalar=1.0, in1=rr_sb[up2][:, 0:R],
                                    op0=ALU.mult, op1=ALU.mult), s_dve, 1)
                                w.wait("vector", s_dve)
                                w.do("vector", lambda e, R=R: e.reduce_sum(
                                    ln_s[:, 0:1], sp_sb[up2][:, 0:R], axis=AX.X), s_dve, 1)
                            else:
                                w.do("vector", lambda e: e.tensor_copy(
                                    out=ln_s[:, 0:1], in_=sp_sb[up2][:, 0:1]), s_dve, 1)
                            w.wait("vector", s_dve)
                            w.do("vector", lambda e: e.reciprocal(
                                rr_sb[up2][:, 5:6], ln_s[:, 0:1]), s_dve, 1)
                            w.wait("vector", s_dve)
                            w.wait("vector", s_pe, diag_rd[up2])
                            if R > 1:
                                w.do("vector", lambda e, R=R: e.tensor_scalar(
                                    out=rr_sb[up2][:, 0:R], in0=rr_sb[up2][:, 0:R],
                                    scalar1=rr_sb[up2][:, 5:6], scalar2=None,
                                    op0=ALU.mult), s_dve, 1)
                                w.wait("vector", s_dve)
                                for r in range(R):
                                    w.do("vector", lambda e, r=r: e.tensor_scalar(
                                        out=diag_sb[up2][:, r * 128:(r + 1) * 128],
                                        in0=ident_bf[:], scalar1=rr_sb[up2][:, r:r + 1],
                                        scalar2=None, op0=ALU.mult),
                                        s_dve if r == R - 1 else None, 1)
                            else:
                                w.do("vector", lambda e: e.tensor_scalar(
                                    out=diag_sb[up2][:, 0:128], in0=ident_bf[:],
                                    scalar1=rr_sb[up2][:, 5:6], scalar2=None,
                                    op0=ALU.mult), s_dve, 1)
                            rr_rd[up2] = w.v(s_dve)
                            plv = w.v(s_dve)

                            def queue_consume(up2=up2, par=par, hp=hp, hp0=hp0,
                                              sl=sl, ss=ss, kl=kl, plv=plv):
                                nt = kl // 128
                                ngr = cdiv(nt, 4)

                                def do_transposes(tg):
                                    half = tg % 2
                                    cnt = min(4, nt - tg * 4)
                                    if tg == 0:
                                        w.wait("tensor", s_dve, plv)
                                    w.wait("tensor", s_act, pt_evac[half])
                                    for j in range(cnt):
                                        tc = tg * 4 + j
                                        r = tc // 8
                                        w.do("tensor", lambda e, half=half, j=j, tc=tc, r=r: e.matmul(
                                            ps_PT[half][:, j * 128:(j + 1) * 128],
                                            P_sb[up2][:, tc * 128:(tc + 1) * 128],
                                            diag_sb[up2][:, r * 128:(r + 1) * 128],
                                            start=True, stop=True),
                                            s_pe, 1)
                                    diag_rd[up2] = w.v(s_pe)
                                    pv = w.v(s_pe)
                                    w.do("scalar", lambda e, half=half, cnt=cnt, pv=pv,
                                         prd=pt_read[half]: (
                                        e.wait_ge(s_pe, max(pv, prd)),
                                        e.activation(out=pt_sb[half][:, 0:cnt * 128],
                                                     in_=ps_PT[half][:, 0:cnt * 128],
                                                     func=AF.Identity, bias=0.0, scale=1.0))[-1],
                                        s_act, 1)
                                    pt_evacd[half] = w.v(s_act)

                                def do_vmms(tg):
                                    half = tg % 2
                                    cnt = min(4, nt - tg * 4)
                                    w.wait("tensor", s_act, pt_evacd[half])
                                    if tg == 0:
                                        w.wait("tensor", s_act, max(aA_rd2[up2], aA_rd))
                                    for j in range(cnt):
                                        tc = tg * 4 + j
                                        w.do("tensor", lambda e, half=half, j=j, tc=tc: e.matmul(
                                            ps_A[0:64, up2 * 512:up2 * 512 + 128],
                                            v_sb[par][:, tc * 128 + hp0:tc * 128 + hp0 + 64],
                                            pt_sb[half][:, j * 128:(j + 1) * 128],
                                            start=(tc == 0), stop=(tc == nt - 1)),
                                            s_pe if tc == nt - 1 else None, 1)
                                    pt_read[half] = w.v(s_pe)
                                    pt_evac[half] = pt_evacd[half]
                                    if tg == ngr - 1:
                                        p_read[up2] = w.v(s_pe)
                                        kvq_rd[par] = w.v(s_pe)
                                        pv2 = w.v(s_pe)
                                        w.wait("scalar", s_pe, pv2)
                                        sspar = ss % 2
                                        w.wait("scalar", s_pe, ao_rd[sspar])
                                        w.do("scalar", lambda e: e.activation(
                                            out=ao_sb[sspar][hp][hp0:hp0 + 64,
                                                                 sl * 128:(sl + 1) * 128],
                                            in_=ps_A[0:64, up2 * 512:up2 * 512 + 128],
                                            func=AF.Identity, bias=0.0, scale=1.0),
                                            s_act, 1)
                                        aA_rd2[up2] = w.v(s_act)

                                def mk_step(k):
                                    def step():
                                        if k < ngr:
                                            do_transposes(k)
                                        if k >= 1:
                                            do_vmms(k - 1)
                                    return step

                                for k in range(ngr + 1):
                                    pending.append((unit_idx, mk_step(k)))

                            flush_steps()
                            queue_consume()
                            unit_idx += 1

                flush_steps()
                sspar = ss % 2
                ao_ready = w.v(s_act)
                if _dbg2 and ss == 0:
                    w.wait("sync", s_act, ao_ready)
                    for c in range(DC):
                        w.do("sync", lambda e, c=c: e.dma_start(
                            out=AOdbg.ap()[c], in_=ao_sb[sspar][c][:]), s_pro, 16)
                    for pp in range(2):
                        w.do("sync", lambda e, pp=pp: e.dma_start(
                            out=Pdbg.ap()[pp], in_=P_sb[pp][:]), s_pro, 16)
                w.wait("tensor", s_wo, cap2["wo"])
                for sl in range(4):
                    qt = 4 * ss + sl
                    xop = 0
                    w.wait("sync", s_dve, xo_rd[xop])
                    w.do("sync", lambda e, qt=qt, xop=xop: e.dma_start(
                        out=xo_f[xop][:], in_=x_own[qt]), s_xo, 16)
                    xo_ready = w.v(s_xo)
                    for dout in range(DC):
                        w.wait("tensor", s_act, ao_ready)
                        w.wait("tensor", s_act)
                        for dc in range(DC):
                            w.do("tensor", lambda e, dc=dc, dout=dout, sl=sl: e.matmul(
                                ps_A[:, 128:256],
                                wo_sb[dc][:, dout * 128:(dout + 1) * 128],
                                ao_sb[sspar][dc][:, sl * 128:(sl + 1) * 128],
                                start=(dc == 0), stop=(dc == DC - 1)),
                                s_pe if dc == DC - 1 else None, 1)
                        pv = w.v(s_pe)
                        w.do("scalar", lambda e, dout=dout, pv=pv: (
                            e.wait_ge(s_pe, pv),
                            e.activation(out=po_sb[:, dout * 128:(dout + 1) * 128],
                                         in_=ps_A[:, 128:256], func=AF.Identity,
                                         bias=0.0, scale=1.0))[-1], s_act, 1)
                    ao_rd[sspar] = w.v(s_pe)
                    for dout in range(DC):
                        w.wait("tensor", s_act)
                        w.do("tensor", lambda e, dout=dout: e.transpose(
                            ps_A[:, 256:384],
                            po_sb[:, dout * 128:(dout + 1) * 128], identf[:]), s_pe, 1)
                        pv = w.v(s_pe)
                        w.do("scalar", lambda e, dout=dout, pv=pv: (
                            e.wait_ge(s_pe, pv),
                            e.activation(out=pt_tok[:, dout * 128:(dout + 1) * 128],
                                         in_=ps_A[:, 256:384], func=AF.Identity,
                                         bias=0.0, scale=1.0))[-1], s_act, 1)
                    av = w.v(s_act)
                    if _dbg2 and ss == 0:
                        w.wait("sync", s_act, av)
                        w.do("sync", lambda e, sl=sl: e.dma_start(
                            out=POdbg.ap()[sl], in_=pt_tok[:]), s_pro, 16)
                    w.wait("vector", s_act, av)
                    w.wait("vector", s_xo, xo_ready)
                    w.wait("vector", s_bo, cap2["bo"])
                    w.do("vector", lambda e, xop=xop: e.tensor_tensor(
                        out=res_tok[:], in0=pt_tok[:], in1=xo_f[xop][:], op=ALU.add), None)
                    w.do("vector", lambda e: e.tensor_tensor(
                        out=res_tok[:], in0=res_tok[:], in1=bo_bc[:], op=ALU.add), s_dve, 1)
                    if _dbg2 and ss == 0:
                        w.wait("sync", s_dve)
                        w.do("sync", lambda e, sl=sl: e.dma_start(
                            out=REdbg.ap()[sl], in_=res_tok[:]), s_pro, 16)
                    xo_rd[xop] = w.v(s_dve) + 1
                    w.do("vector", lambda e: e.reduce_sum(
                        ln_s[:, 0:1], res_tok[:], axis=AX.X), s_dve, 1)
                    dv = w.v(s_dve)
                    w.do("scalar", lambda e, dv=dv: (
                        e.wait_ge(s_dve, dv),
                        e.activation(out=po_sb[:], in_=res_tok[:], func=AF.Square,
                                     accum_out=ln_s[:, 1:2]))[-1], s_act, 1)
                    av = w.v(s_act)
                    w.wait("vector", s_act, av)
                    w.wait("vector", s_dve)
                    w.do("vector", lambda e: e.tensor_scalar(
                        out=ln_s[:, 2:3], in0=ln_s[:, 0:1], scalar1=1.0 / D,
                        scalar2=None, op0=ALU.mult), s_dve, 1)
                    w.do("vector", lambda e: e.tensor_scalar(
                        out=ln_s[:, 3:4], in0=ln_s[:, 1:2], scalar1=1.0 / D,
                        scalar2=None, op0=ALU.mult), s_dve, 1)
                    w.wait("vector", s_dve)
                    w.do("vector", lambda e: e.tensor_tensor(
                        out=ln_s[:, 4:5], in0=ln_s[:, 2:3], in1=ln_s[:, 2:3], op=ALU.mult), s_dve, 1)
                    w.wait("vector", s_dve)
                    w.do("vector", lambda e: e.tensor_tensor(
                        out=ln_s[:, 5:6], in0=ln_s[:, 3:4], in1=ln_s[:, 4:5], op=ALU.subtract), s_dve, 1)
                    dv = w.v(s_dve)
                    w.do("scalar", lambda e, dv=dv: (
                        e.wait_ge(s_dve, dv),
                        e.activation(out=ln_s[:, 6:7], in_=ln_s[:, 5:6], func=AF.Ln,
                                     bias=eps_sb[:], scale=1.0))[-1], None)
                    w.do("scalar", lambda e: e.activation(
                        out=ln_s[:, 7:8], in_=ln_s[:, 6:7], func=AF.Exp, bias=0.0,
                        scale=-0.5), s_act, 1)
                    av = w.v(s_act)
                    w.wait("vector", s_act, av)
                    w.do("vector", lambda e: e.tensor_scalar(
                        out=xn_tok[:], in0=res_tok[:], scalar1=ln_s[:, 2:3],
                        scalar2=ln_s[:, 7:8], op0=ALU.subtract, op1=ALU.mult), s_dve, 1)
                    dv = w.v(s_dve)
                    w.do("scalar", lambda e, dv=dv: (
                        e.wait_ge(s_dve, dv),
                        e.activation(out=xn_bf[:], in_=xn_tok[:], func=AF.Identity,
                                     bias=0.0, scale=1.0))[-1], s_act, 1)
                    av = w.v(s_act)
                    w.wait("scalar", s_xnb, cap2.get("xnbf", 0))
                    for c in range(DC):
                        half = c % 2
                        w.wait("tensor", s_act, max(av, pt_evac[half]))
                        w.do("tensor", lambda e, c=c, half=half: e.matmul(
                            ps_PT[half][:, 0:128],
                            xn_bf[:, c * 128:(c + 1) * 128], ident_bf[:],
                            start=True, stop=True), s_pe, 1)
                        pv = w.v(s_pe)
                        w.do("scalar", lambda e, c=c, qt=qt, half=half, pv=pv: (
                            e.wait_ge(s_pe, pv),
                            e.activation(out=xnbf_stage[:, c * 128:(c + 1) * 128],
                                         in_=ps_PT[half][:, 0:128],
                                         func=AF.Identity, bias=0.0, scale=1.0))[-1], s_act, 1)
                        pt_evac[half] = w.v(s_act)
                    av2 = w.v(s_act)
                    w.wait("sync", s_act, av2)
                    w.do("sync", lambda e, qt=qt: e.dma_start(
                        out=XNBFscr.ap()[qt].rearrange("c p t -> p c t"),
                        in_=xnbf_stage[:].rearrange("p (c t) -> p c t", c=DC)), s_xnb, 16)
                    cap2["xnbf"] = w.v(s_xnb)
                    w.wait("scalar", s_xnf, cap2.get("xnfm", 0))
                    for c in range(DC):
                        w.wait("tensor", s_act)
                        w.do("tensor", lambda e, c=c: e.transpose(
                            ps_A[:, 256:384], xn_tok[:, c * 128:(c + 1) * 128], identf[:]),
                            s_pe, 1)
                        pv = w.v(s_pe)
                        w.do("scalar", lambda e, c=c, pv=pv: (
                            e.wait_ge(s_pe, pv),
                            e.activation(out=xn_fm[:, c * 128:(c + 1) * 128],
                                         in_=ps_A[:, 256:384], func=AF.Identity,
                                         bias=0.0, scale=1.0))[-1], s_act, 1)
                    av = w.v(s_act)
                    w.wait("sync", s_act, av)
                    w.do("sync", lambda e, qt=qt: e.dma_start(
                        out=XNscr.ap()[qt].rearrange("c p t -> p c t"),
                        in_=xn_fm[:].rearrange("p (c t) -> p c t", c=DC)), s_xnf, 16)
                    cap2["xnfm"] = w.v(s_xnf)
                aA_rd = w.v(s_act)
            w.wait("sync", s_xnf)
            w.wait("sync", s_xnb)

        run_phase(phase2)
        p2.close()
        if "3" not in _ph:
            return nc

        p3 = ExitStack()
        xnT = [p3.enter_context(nc.sbuf_tensor(f"xnT{c}", [128, NSLOT * 128], BF16))
               for c in range(DC)]
        acc = [p3.enter_context(nc.sbuf_tensor(f"acc{c}", [128, 1024], F32)) for c in range(DC)]
        hT = [p3.enter_context(nc.sbuf_tensor(f"hT{c}", [128, 1024], BF16)) for c in range(HCN)]
        up_sb = [p3.enter_context(nc.sbuf_tensor(f"up{p}", [128, 4096], BF16)) for p in range(3)]
        dn_sb = [p3.enter_context(nc.sbuf_tensor(f"dn{p}", [128, 4096], BF16)) for p in range(3)]
        cwT = p3.enter_context(nc.sbuf_tensor("cwT", [16, 2048], BF16))
        cwrow = [p3.enter_context(nc.sbuf_tensor(f"cwrow{i}", [1, 1024], BF16)) for i in range(2)]
        cwbc = [p3.enter_context(nc.sbuf_tensor(f"cwbc{p}", [128, 1024], BF16)) for p in range(2)]
        ysb = [p3.enter_context(nc.sbuf_tensor(f"ysb{p}", [128, 512], F32)) for p in range(2)]
        g_sb = [p3.enter_context(nc.sbuf_tensor(f"g{j}", [128, E], F32)) for j in range(4)]
        gm = p3.enter_context(nc.sbuf_tensor("gm", [128, 8], F32))
        cw_f = p3.enter_context(nc.sbuf_tensor("cw_f", [128, E], F32))
        r2_tok = p3.enter_context(nc.sbuf_tensor("r2_tok", [128, D], F32))
        o_tok = [p3.enter_context(nc.sbuf_tensor(f"otok{p}", [128, D], F32)) for p in range(2)]
        ln3 = p3.enter_context(nc.sbuf_tensor("ln3", [128, 8], F32))
        PS8 = p3.enter_context(nc.psum_tensor("PS8", [128, 8 * 512], F32))

        def phase3(w):
            cap3 = {}
            for c in range(DC):
                w.do("sync", lambda e, c=c: e.dma_start(
                    out=xnT[c][:].rearrange("p (qt t) -> p qt t", t=128),
                    in_=XNBFscr.ap()[:, c].rearrange("qt p t -> p qt t")), s_xnt, 16)
            xnT_ready = w.v(s_xnt)
            w.wait("tensor", s_xnt, xnT_ready)
            w.do("gpsimd", lambda e: e.memset(cwT[:], 0.0), s_pool, 1)
            hT_rd = [0] * HCN
            acc_wr_last = [0] * DC
            up_rd = [0, 0, 0]
            dn_rd = [0, 0, 0]
            ysb_rd = [0, 0]
            cwbc_rd = [0, 0]
            acc_pe_rd = 0
            otok_rd = [0, 0]
            upbank_rd = [0] * 6
            ybank_rd = [0] * 8

            for tile in range(2):
                t0 = tile * 1024
                if tile == 1:
                    w.wait("sync", s_ws)
                w.wait("sync", s_pe, acc_pe_rd)
                w.wait("sync", s_dve, cap3.get("acc_dve", 0))
                for c in range(DC):
                    w.do("sync", lambda e, c=c, tile=tile: e.dma_start(
                        out=acc[c][:].rearrange("p (qt t) -> p qt t", t=128),
                        in_=XNscr.ap()[tile * 8:(tile + 1) * 8, c].rearrange("qt p t -> p qt t")),
                        s_acc, 16)
                acc_ready = w.v(s_acc)

                for sl8 in range(8):
                    qt = tile * 8 + sl8
                    if sl8 == 0:
                        w.wait("scalar", s_pool)
                        w.wait("tensor", s_act)
                        w.wait("tensor", s_acc, acc_ready)
                    w.wait("tensor", s_act, cap3.get("gate_act", 0))
                    for dc in range(DC):
                        w.do("tensor", lambda e, dc=dc, sl8=sl8: e.matmul(
                            PS8[:, 0:E], acc[dc][:, sl8 * 128:(sl8 + 1) * 128],
                            wg_sb[:, dc * E:(dc + 1) * E],
                            start=(dc == 0), stop=(dc == DC - 1)),
                            s_pe if dc == DC - 1 else None, 1)
                    pv = w.v(s_pe)
                    w.wait("vector", s_pe, pv)
                    def gstep(fn):
                        w.wait("vector", s_dve)
                        w.do("vector", fn, s_dve, 1)
                    gstep(lambda e: e.reduce_max(gm[:, 0:1], PS8[:, 0:E], axis=AX.X))
                    gstep(lambda e: e.tensor_scalar(
                        out=g_sb[0][:], in0=PS8[:, 0:E], scalar1=gm[:, 0:1], scalar2=None,
                        op0=ALU.is_ge))
                    gstep(lambda e: e.scalar_tensor_tensor(
                        out=g_sb[1][:], in0=g_sb[0][:], scalar=NEG, in1=PS8[:, 0:E],
                        op0=ALU.mult, op1=ALU.add))
                    gstep(lambda e: e.reduce_max(gm[:, 1:2], g_sb[1][:], axis=AX.X))
                    gstep(lambda e: e.tensor_scalar(
                        out=g_sb[2][:], in0=g_sb[1][:], scalar1=gm[:, 1:2], scalar2=None,
                        op0=ALU.is_ge))
                    gstep(lambda e: e.scalar_tensor_tensor(
                        out=g_sb[3][:], in0=g_sb[2][:], scalar=NEG, in1=g_sb[1][:],
                        op0=ALU.mult, op1=ALU.add))
                    gstep(lambda e: e.reduce_max(gm[:, 2:3], g_sb[3][:], axis=AX.X))
                    gstep(lambda e: e.tensor_scalar(
                        out=gm[:, 3:4], in0=gm[:, 0:1], scalar1=-1.0, scalar2=None,
                        op0=ALU.mult))
                    gstep(lambda e: e.tensor_scalar(
                        out=g_sb[0][:], in0=PS8[:, 0:E], scalar1=gm[:, 2:3], scalar2=None,
                        op0=ALU.is_ge))
                    dv = w.v(s_dve)
                    w.do("scalar", lambda e, dv=dv: (
                        e.wait_ge(s_dve, dv),
                        e.activation(out=g_sb[1][:], in_=PS8[:, 0:E], func=AF.Exp,
                                     bias=gm[:, 3:4], scale=1.0))[-1], s_act, 1)
                    av = w.v(s_act)
                    w.wait("vector", s_act, av)
                    gstep(lambda e: e.tensor_tensor(
                        out=g_sb[2][:], in0=g_sb[1][:], in1=g_sb[0][:], op=ALU.mult))
                    gstep(lambda e: e.reduce_sum(gm[:, 4:5], g_sb[2][:], axis=AX.X))
                    gstep(lambda e: e.reciprocal(gm[:, 5:6], gm[:, 4:5]))
                    gstep(lambda e: e.tensor_scalar(
                        out=cw_f[:], in0=g_sb[2][:], scalar1=gm[:, 5:6], scalar2=None,
                        op0=ALU.mult))
                    w.wait("vector", s_dve)
                    w.do("vector", lambda e: e.engine_nop(), s_dve, 1)
                    dv = w.v(s_dve)
                    w.wait("tensor", s_dve, dv)
                    w.wait("tensor", s_act)
                    w.do("tensor", lambda e: e.transpose(
                        PS8[0:E, 512:640], cw_f[:], identf[:]), s_pe, 1)
                    pv = w.v(s_pe)
                    w.do("scalar", lambda e, qt=qt, pv=pv: (
                        e.wait_ge(s_pe, pv),
                        e.activation(out=cwT[0:E, qt * 128:(qt + 1) * 128],
                                     in_=PS8[0:E, 512:640], func=AF.Identity,
                                     bias=0.0, scale=1.0))[-1], s_act, 1)
                    cap3["gate_act"] = w.v(s_act)

                for e_i in range(int(os.environ.get("MK_P3E", str(E)))):
                    epar = e_i % 2
                    if e_i == 0:
                        w.wait("sync", s_act, cap3.get("gate_act", 0))
                        w.wait("sync", s_pe, cap3.get(("cwrow_pe", 0), 0))
                        w.do("sync", lambda e, tile=tile: e.dma_start(
                            out=cwrow[0][:], in_=cwT[0:1, tile * 1024:(tile + 1) * 1024]),
                            s_cw, 16)
                        cap3[("cwr", 0)] = w.v(s_cw)
                    if e_i + 1 < E:
                        npar = (e_i + 1) % 2
                        w.wait("sync", s_pe, cap3.get(("cwrow_pe", npar), 0))
                        w.do("sync", lambda e, e_i=e_i, tile=tile, npar=npar: e.dma_start(
                            out=cwrow[npar][:],
                            in_=cwT[e_i + 1:e_i + 2, tile * 1024:(tile + 1) * 1024]),
                            s_cw, 16)
                        cap3[("cwr", npar)] = w.v(s_cw)
                    w.wait("tensor", s_cw, cap3[("cwr", epar)])
                    w.wait("tensor", s_dve, cwbc_rd[epar])
                    w.wait("tensor", s_act, max(ybank_rd[6], ybank_rd[7]))
                    for j in range(2):
                        w.do("tensor", lambda e, j=j, epar=epar: e.matmul(
                            PS8[:, 6 * 512 + j * 512:6 * 512 + (j + 1) * 512],
                            ones_bf[:], cwrow[epar][0:1, j * 512:(j + 1) * 512],
                            start=True, stop=True), s_pe, 1)
                    cap3[("cwrow_pe", epar)] = w.v(s_pe)
                    pv = w.v(s_pe)
                    w.wait("scalar", s_dve, cwbc_rd[epar])
                    w.do("scalar", lambda e, epar=epar, pv=pv: (
                        e.wait_ge(s_pe, pv),
                        e.activation(out=cwbc[epar][:], in_=PS8[:, 6 * 512:8 * 512],
                                     func=AF.Identity, bias=0.0, scale=1.0))[-1], s_act, 1)
                    ybank_rd[6] = w.v(s_act)
                    ybank_rd[7] = w.v(s_act)
                    cwbc_ready = w.v(s_act)

                    for hcg in range(8):
                        spar = hcg % 3
                        w.wait("gpsimd" if tile == 0 else "sync", s_pe, up_rd[spar])
                        if tile == 0:
                            w.wait("gpsimd", s_ws, cap3.get(("upw", spar), 0))
                            w.do("gpsimd", lambda e, e_i=e_i, hcg=hcg, spar=spar: e.dma_start(
                                out=up_sb[spar][:].rearrange("p (dc hch) -> p dc hch", dc=DC),
                                in_=Wup_t.ap()[e_i].rearrange(
                                    "(dc p) (hcg hch) -> hcg p dc hch",
                                    p=128, hch=512)[hcg]), s_upg[spar], 16)
                            slab_ready = w.v(s_upg[spar])
                            w.wait("sync", s_upg[spar], slab_ready)
                            w.do("sync", lambda e, e_i=e_i, hcg=hcg, spar=spar: e.dma_start(
                                out=WUPscr.ap()[e_i, hcg], in_=up_sb[spar][:]), s_ws, 16)
                            cap3[("upw", spar)] = w.v(s_ws)
                            slab_sem, slab_val = s_upg[spar], slab_ready
                        else:
                            w.do("sync", lambda e, e_i=e_i, hcg=hcg, spar=spar: e.dma_start(
                                out=up_sb[spar][:], in_=WUPscr.ap()[e_i, hcg]), s_upr[spar], 16)
                            slab_sem, slab_val = s_upr[spar], w.v(s_upr[spar])
                        for hc in range(4):
                            hchunk = hcg * 4 + hc
                            for ts_ in range(2):
                                bank = (hchunk * 2 + ts_) % 6
                                w.wait("tensor", slab_sem, slab_val)
                                w.wait("tensor", s_act, upbank_rd[bank])
                                w.wait("scalar", s_pe, hT_rd[hchunk])
                                for dc in range(DC):
                                    w.do("tensor", lambda e, dc=dc, hc=hc, ts_=ts_, bank=bank, spar=spar, t0=t0: e.matmul(
                                        PS8[:, bank * 512:(bank + 1) * 512],
                                        up_sb[spar][:, dc * 512 + hc * 128:dc * 512 + (hc + 1) * 128],
                                        xnT[dc][:, t0 + ts_ * 512:t0 + (ts_ + 1) * 512],
                                        start=(dc == 0), stop=(dc == DC - 1)),
                                        s_pe if dc == DC - 1 else None, 1)
                                pv = w.v(s_pe)
                                w.do("scalar", lambda e, hchunk=hchunk, ts_=ts_, bank=bank, e_i=e_i, pv=pv: (
                                    e.wait_ge(s_pe, pv),
                                    e.activation(out=hT[hchunk][:, ts_ * 512:(ts_ + 1) * 512],
                                                 in_=PS8[:, bank * 512:(bank + 1) * 512],
                                                 func=AF.Gelu,
                                                 bias=bup_c[:, e_i * 32 + hchunk:e_i * 32 + hchunk + 1],
                                                 scale=1.0))[-1], s_act, 1)
                                upbank_rd[bank] = w.v(s_act)
                        up_rd[spar] = w.v(s_pe)
                    gelu_done = w.v(s_act)

                    for half in range(2):
                        for hq in range(8):
                            spar = hq % 3
                            first_visit = (tile == 0 and half == 0)
                            role = "gpsimd" if first_visit else "sync"
                            w.wait(role, s_pe, dn_rd[spar])
                            if first_visit:
                                w.wait("gpsimd", s_ws, cap3.get(("dnw", spar), 0))
                                w.do("gpsimd", lambda e, e_i=e_i, hq=hq, spar=spar: e.dma_start(
                                    out=dn_sb[spar][:].rearrange("p (hc d) -> p hc d", hc=4),
                                    in_=Wdn_t.ap()[e_i].rearrange(
                                        "(hq hc p) d -> hq p hc d", hc=4, p=128)[hq]), s_dng[spar], 16)
                                slab_ready = w.v(s_dng[spar])
                                w.wait("sync", s_dng[spar], slab_ready)
                                w.do("sync", lambda e, e_i=e_i, hq=hq, spar=spar: e.dma_start(
                                    out=WDNscr.ap()[e_i, hq], in_=dn_sb[spar][:]), s_ws, 16)
                                cap3[("dnw", spar)] = w.v(s_ws)
                                cap3[("dnwv", e_i, hq)] = w.v(s_ws)
                                slab_sem, slab_val = s_dng[spar], slab_ready
                            else:
                                w.wait("sync", s_ws, cap3.get(("dnwv", e_i, hq), 0))
                                w.do("sync", lambda e, e_i=e_i, hq=hq, spar=spar: e.dma_start(
                                    out=dn_sb[spar][:], in_=WDNscr.ap()[e_i, hq]), s_dnr[spar], 16)
                                slab_sem, slab_val = s_dnr[spar], w.v(s_dnr[spar])
                            for hc in range(4):
                                hchunk = hq * 4 + hc
                                w.wait("tensor", slab_sem, slab_val)
                                if hchunk == 0:
                                    w.wait("tensor", s_act, gelu_done)
                                    for b in range(8):
                                        w.wait("tensor", s_act, ybank_rd[b])
                                for dout in range(DC):
                                    w.do("tensor", lambda e, hc=hc, dout=dout, half=half, spar=spar, hchunk=hchunk: e.matmul(
                                        PS8[:, dout * 512:(dout + 1) * 512],
                                        dn_sb[spar][:, (hc * DC + dout) * 128:(hc * DC + dout + 1) * 128],
                                        hT[hchunk][:, half * 512:(half + 1) * 512],
                                        start=(hchunk == 0), stop=(hchunk == HCN - 1)),
                                        s_pe if (hc == 3 and dout == DC - 1) else None, 1)
                            hq_done = w.v(s_pe)
                            dn_rd[spar] = hq_done
                            if half == 1:
                                for hc in range(4):
                                    hT_rd[hq * 4 + hc] = hq_done
                        pv = w.v(s_pe)
                        for dout in range(DC):
                            yp = dout % 2
                            w.wait("scalar", s_pe, pv)
                            w.wait("scalar", s_dve, ysb_rd[yp])
                            w.do("scalar", lambda e, dout=dout, yp=yp, e_i=e_i: e.activation(
                                out=ysb[yp][:], in_=PS8[:, dout * 512:(dout + 1) * 512],
                                func=AF.Identity,
                                bias=bdn_c[:, e_i * 8 + dout:e_i * 8 + dout + 1], scale=1.0),
                                s_act, 1)
                            ybank_rd[dout] = w.v(s_act)
                            av = w.v(s_act)
                            w.wait("vector", s_act, max(av, cwbc_ready))
                            if e_i == 0 and half == 0 and dout == 0:
                                w.wait("vector", s_acc, acc_ready)
                            w.do("vector", lambda e, yp=yp, epar=epar, half=half: e.tensor_tensor(
                                out=ysb[yp][:], in0=ysb[yp][:],
                                in1=cwbc[epar][:, half * 512:(half + 1) * 512], op=ALU.mult), None)
                            w.do("vector", lambda e, dout=dout, yp=yp, half=half: e.tensor_tensor(
                                out=acc[dout][:, half * 512:(half + 1) * 512],
                                in0=acc[dout][:, half * 512:(half + 1) * 512],
                                in1=ysb[yp][:], op=ALU.add), s_dve, 1)
                            ysb_rd[yp] = w.v(s_dve)
                            if e_i == E - 1:
                                acc_wr_last[dout] = max(acc_wr_last[dout], w.v(s_dve))
                    cwbc_rd[epar] = w.v(s_dve)
                    for b in range(6):
                        upbank_rd[b] = max(upbank_rd[b], ybank_rd[b])

                if _dbg3 and tile == 0:
                    w.wait("sync", s_dve)
                    for c in range(DC):
                        w.do("sync", lambda e, c=c: e.dma_start(
                            out=ACCdbg.ap()[c], in_=acc[c][:]), s_pro, 16)
                    w.wait("sync", s_act)
                    w.do("sync", lambda e: e.dma_start(out=CWdbg.ap(), in_=cwT[0:16, :]), s_pro, 16)
                    for c in range(4):
                        w.do("sync", lambda e, c=c: e.dma_start(
                            out=HTdbg.ap()[c], in_=hT[c][:]), s_pro, 16)
                cap3["acc_dve"] = w.v(s_dve)
                for sl8 in range(8):
                    qt = tile * 8 + sl8
                    for c in range(DC):
                        w.wait("tensor", s_dve, acc_wr_last[c])
                        w.wait("tensor", s_act)
                        w.do("tensor", lambda e, c=c, sl8=sl8: e.transpose(
                            PS8[:, 0:128], acc[c][:, sl8 * 128:(sl8 + 1) * 128], identf[:]),
                            s_pe, 1)
                        pv = w.v(s_pe)
                        w.do("scalar", lambda e, c=c, pv=pv,
                             prd=cap3.get("otok_dve", 0): (
                            e.wait_ge(s_pe, pv),
                            e.wait_ge(s_dve, prd),
                            e.activation(out=r2_tok[:, c * 128:(c + 1) * 128],
                                         in_=PS8[:, 0:128], func=AF.Identity,
                                         bias=0.0, scale=1.0))[-1], s_act, 1)
                    av = w.v(s_act)
                    w.wait("vector", s_act, av)
                    w.do("vector", lambda e: e.reduce_sum(
                        ln3[:, 0:1], r2_tok[:], axis=AX.X), s_dve, 1)
                    dv = w.v(s_dve)
                    op = sl8 % 2
                    w.do("scalar", lambda e, dv=dv, op=op, prd=otok_rd[op]: (
                        e.wait_ge(s_dve, dv),
                        e.wait_ge(s_out, prd),
                        e.activation(out=o_tok[op][:], in_=r2_tok[:], func=AF.Square,
                                     accum_out=ln3[:, 1:2]))[-1], s_act, 1)
                    av = w.v(s_act)
                    w.wait("vector", s_act, av)
                    w.wait("vector", s_dve)
                    w.do("vector", lambda e: e.tensor_scalar(
                        out=ln3[:, 2:3], in0=ln3[:, 0:1], scalar1=1.0 / D, scalar2=None,
                        op0=ALU.mult), s_dve, 1)
                    w.do("vector", lambda e: e.tensor_scalar(
                        out=ln3[:, 3:4], in0=ln3[:, 1:2], scalar1=1.0 / D, scalar2=None,
                        op0=ALU.mult), s_dve, 1)
                    w.wait("vector", s_dve)
                    w.do("vector", lambda e: e.tensor_tensor(
                        out=ln3[:, 4:5], in0=ln3[:, 2:3], in1=ln3[:, 2:3], op=ALU.mult), s_dve, 1)
                    w.wait("vector", s_dve)
                    w.do("vector", lambda e: e.tensor_tensor(
                        out=ln3[:, 5:6], in0=ln3[:, 3:4], in1=ln3[:, 4:5], op=ALU.subtract), s_dve, 1)
                    dv = w.v(s_dve)
                    w.do("scalar", lambda e, dv=dv: (
                        e.wait_ge(s_dve, dv),
                        e.activation(out=ln3[:, 6:7], in_=ln3[:, 5:6], func=AF.Ln,
                                     bias=eps_sb[:], scale=1.0))[-1], None)
                    w.do("scalar", lambda e: e.activation(
                        out=ln3[:, 7:8], in_=ln3[:, 6:7], func=AF.Exp, bias=0.0,
                        scale=-0.5), s_act, 1)
                    av = w.v(s_act)
                    w.wait("vector", s_act, av)
                    w.wait("vector", s_out, otok_rd[op])
                    w.do("vector", lambda e, op=op: e.tensor_scalar(
                        out=o_tok[op][:], in0=r2_tok[:], scalar1=ln3[:, 2:3],
                        scalar2=ln3[:, 7:8], op0=ALU.subtract, op1=ALU.mult), s_dve, 1)
                    dv = w.v(s_dve)
                    cap3["otok_dve"] = dv
                    w.wait("sync", s_dve, dv)
                    w.do("sync", lambda e, qt=qt, op=op: e.dma_start(
                        out=out_ap[qt], in_=o_tok[op][:]), s_out, 16)
                    otok_rd[op] = w.v(s_out)
                acc_pe_rd = w.v(s_pe)
            for _s in (s_pro, s_bias, s_wslab, s_gx[0], s_gx[1], s_kst[0],
                       s_kst[1], s_vst[0], s_vst[1], s_kr[0], s_kr[1], s_xo,
                       s_bo, s_wo, s_xnf, s_xnb, s_xnt, s_acc, s_ws,
                       s_upg[0], s_upg[1], s_upg[2], s_upr[0], s_upr[1],
                       s_upr[2], s_dng[0], s_dng[1], s_dng[2],
                       s_dnr[0], s_dnr[1], s_dnr[2], s_cw):
                w.wait("sync", _s)
            w.wait("sync", s_out, 16 * NSLOT)

        run_phase(phase3)
        p3.close()

    return nc


_NC_CACHE = []


def kernel(**inputs):
    x = np.ascontiguousarray(np.asarray(inputs["L0_attn_x"], dtype=np.float32))
    weights = {
        "Wq": inputs["L0_attn_Wq"], "bq": inputs["L0_attn_bq"],
        "Wk": inputs["L0_attn_Wk"], "bk": inputs["L0_attn_bk"],
        "Wv": inputs["L0_attn_Wv"], "bv": inputs["L0_attn_bv"],
        "Wo": inputs["L0_attn_Wo"], "bo": inputs["L0_attn_bo"],
        "Wg": inputs["L0_ffn_Wg"], "Wup": inputs["L0_ffn_Wup"],
        "bup": inputs["L0_ffn_bup"], "Wdn": inputs["L0_ffn_Wdown"],
        "bdn": inputs["L0_ffn_bdown"],
    }
    weights = {k: np.ascontiguousarray(np.asarray(v, dtype=np.float32))
               for k, v in weights.items()}
    for k in ("bq", "bk", "bv", "bo"):
        weights[k] = weights[k].reshape(1, D)

    in_maps = []
    slot_lists = []
    for c in range(8):
        b = c // 2
        slots = SLOTS_EVEN if c % 2 == 0 else SLOTS_ODD
        slot_lists.append(slots)
        xo = np.stack([x[b, i * 128:(i + 1) * 128, :] for i in slots])
        m = dict(weights)
        m["x_own"] = np.ascontiguousarray(xo)
        m["x_ctx"] = np.ascontiguousarray(x[b])
        m["rowoff"] = np.array([[128.0 * i for i in slots]], dtype=np.float32)
        m["rowoffn"] = np.array([[-128.0 * i for i in slots]], dtype=np.float32)
        in_maps.append(m)

    if not _NC_CACHE:
        _NC_CACHE.append(build_nc())
    nc = _NC_CACHE[0]
    res = run_bass_kernel_spmd(nc, in_maps, core_ids=list(range(8)))

    y = np.empty((4, 4096, D), dtype=np.float32)
    for c in range(8):
        b = c // 2
        o = np.asarray(res.results[c]["out"], dtype=np.float32)
        for si, i in enumerate(slot_lists[c]):
            y[b, i * 128:(i + 1) * 128, :] = o[si]
    return y


if __name__ == "__main__":
    rng = np.random.default_rng(0)
    ins = {
        "L0_attn_x": rng.standard_normal((4, 4096, 1024), dtype=np.float32),
    }
    for n, sh in (("Wq", (D, D)), ("Wk", (D, D)), ("Wv", (D, D)), ("Wo", (D, D)),
                  ("bq", (D,)), ("bk", (D,)), ("bv", (D,)), ("bo", (D,))):
        ins[f"L0_attn_{n}"] = (rng.standard_normal(sh, dtype=np.float32) / 32.0)
    ins["L0_ffn_Wg"] = rng.standard_normal((D, E), dtype=np.float32) * 0.02
    ins["L0_ffn_Wup"] = rng.standard_normal((E, D, HF), dtype=np.float32) * 0.02
    ins["L0_ffn_bup"] = rng.standard_normal((E, HF), dtype=np.float32) * 0.02
    ins["L0_ffn_Wdown"] = rng.standard_normal((E, HF, D), dtype=np.float32) * 0.02
    ins["L0_ffn_bdown"] = rng.standard_normal((E, D), dtype=np.float32) * 0.02
    out = kernel(**ins)
    print("ok", out.shape, out.dtype, np.abs(out).mean())
